# version 11
# baseline (speedup 1.0000x reference)
import numpy as np
import ml_dtypes
from contextlib import ExitStack

import concourse.bass as bass
import concourse.mybir as mybir
from concourse.bass_utils import run_bass_kernel_spmd

F32 = mybir.dt.float32
BF16 = mybir.dt.bfloat16
U8 = mybir.dt.uint8
AF = mybir.ActivationFunctionType
ALU = mybir.AluOpType
AX = mybir.AxisListType

BF = ml_dtypes.bfloat16

SHIFT = 18.0
EPS = 1e-5


def _build_tables(g=(1.0, 1.0, 1.0)):
    dim = len(g)
    n = 2 ** dim
    masks = sorted(range(n), key=lambda m: (bin(m).count('1'), m))
    inv = {m: i for i, m in enumerate(masks)}
    C = np.zeros((n, n, n), dtype=np.float32)
    for i, a in enumerate(masks):
        for j, b in enumerate(masks):
            sw, at = (0, a >> 1)
            while at:
                sw += bin(at & b).count('1')
                at >>= 1
            s = -1.0 if sw & 1 else 1.0
            for t in range(dim):
                if a >> t & 1 and b >> t & 1:
                    s *= float(g[t])
            C[i, j, inv[a ^ b]] = s
    grades = np.array([bin(m).count('1') for m in masks])
    rev = np.where(grades * (grades - 1) // 2 % 2 == 1, -1.0, 1.0).astype(np.float32)
    met = np.array([float(np.prod([g[t] for t in range(dim) if m >> t & 1] + [1.0]))
                    for m in masks], dtype=np.float32)
    return C, rev, met


def make_cfg(D=4096, L=2048, NH=16, CD=32):
    NB = 8
    cfg = dict(D=D, L=L, NH=NH, CD=CD, NB=NB, NE=8, R=16, SCALING=2.0)
    cfg["HCH"] = CD * NB
    cfg["CBH"] = cfg["HCH"] // 128
    cfg["VW"] = cfg["HCH"] + 1
    cfg["NHL"] = NH // 2
    cfg["CW"] = cfg["NHL"] * cfg["HCH"]
    cfg["ncbl"] = cfg["CW"] // 128
    cfg["nk"] = D // 128
    cfg["ntb"] = L // 128
    cfg["TCQ"] = min(512, L)
    cfg["NQC"] = L // cfg["TCQ"]
    cfg["QBC"] = cfg["TCQ"] // 128
    cfg["TCK"] = min(512, L)
    cfg["NTCK"] = L // cfg["TCK"]
    cfg["VCH"] = min(256, cfg["CW"])
    cfg["NVCHL"] = cfg["CW"] // cfg["VCH"]
    cfg["OW"] = D // 2
    cfg["OCH"] = min(256, cfg["OW"])
    cfg["NOCH"] = cfg["OW"] // cfg["OCH"]
    cfg["NPASS"] = max(1, min(8, L // 256))
    cfg["LF"] = L // cfg["NPASS"]
    cfg["ntbf"] = cfg["LF"] // 128
    cfg["TCF"] = min(256, cfg["LF"])
    cfg["NQCF"] = cfg["LF"] // cfg["TCF"]
    cfg["NBS"] = max(1, D // 512)
    return cfg


def host_consts(cfg):
    C, rev, met = _build_tables()
    NHL, NB = cfg["NHL"], cfg["NB"]
    scale = 1.0 / np.sqrt(cfg["CD"] * NB)
    biv = C[:, :, [4, 5, 6]]
    bkt3 = np.ascontiguousarray(biv.transpose(2, 0, 1).reshape(3, NB * NB)).astype(np.float32)
    revd = np.tile(np.repeat(rev, NB)[None, :], (NHL, 1)).astype(np.float32)
    d1row = np.zeros(NB * NB, dtype=np.float32)
    for i in range(NB):
        d1row[i * NB + i] = rev[i] * scale * met[i]
    d1 = np.tile(d1row[None, :], (NHL, 1)).astype(np.float32)
    mask = np.zeros((128, 128), dtype=np.float32)
    for a in range(128):
        mask[a, (a // NB) * NB:(a // NB) * NB + NB] = 1.0
    rep = np.zeros((NB, 128), dtype=np.float32)
    for a in range(128):
        rep[a % NB, a] = 1.0
    exp8 = np.zeros((cfg["NE"], 128), dtype=np.float32)
    for er in range(cfg["NE"] * cfg["R"]):
        exp8[er // cfg["R"], er] = cfg["SCALING"]
    return dict(
        ibf=np.eye(128, dtype=np.float32).astype(BF),
        if32=np.eye(128, dtype=np.float32),
        rep=rep, mask=mask, exp8=exp8,
        row0=np.concatenate([np.ones((1, 128), dtype=np.float32),
                             np.zeros((127, 128), dtype=np.float32)]).astype(BF),
        bkt3=bkt3, revd=revd, d1=d1,
        ceps=np.full((128, 1), EPS, dtype=np.float32),
        cshift=np.full((128, 1), -SHIFT, dtype=np.float32),
    )


def build_program(cfg):
    D, L = cfg["D"], cfg["L"]
    NHL, HCH, CBH, VW = cfg["NHL"], cfg["HCH"], cfg["CBH"], cfg["VW"]
    NB, NE, CW = cfg["NB"], cfg["NE"], cfg["CW"]
    nk, ncbl, ntb = cfg["nk"], cfg["ncbl"], cfg["ntb"]
    TCQ, NQC, QBC = cfg["TCQ"], cfg["NQC"], cfg["QBC"]
    TCK, NTCK = cfg["TCK"], cfg["NTCK"]
    VCH, NVCHL = cfg["VCH"], cfg["NVCHL"]
    OW, OCH, NOCH = cfg["OW"], cfg["OCH"], cfg["NOCH"]
    LF, ntbf, TCF, NQCF = cfg["LF"], cfg["ntbf"], cfg["TCF"], cfg["NQCF"]
    NPASS = cfg["NPASS"]
    NTBH = ntb // 2
    NBS = cfg["NBS"]
    NG = nk // 4

    nc = bass.Bass("TRN2", target_bir_lowering=False)

    xc = nc.dram_tensor("xc", [L, D], F32, kind="ExternalInput").ap()
    xres = nc.dram_tensor("xres", [L, OW], F32, kind="ExternalInput").ap()
    wqt = nc.dram_tensor("wqt", [ncbl, 128, nk, 128], F32, kind="ExternalInput").ap()
    wkt = nc.dram_tensor("wkt", [ncbl, 128, nk, 128], F32, kind="ExternalInput").ap()
    wvt = nc.dram_tensor("wvt", [NVCHL, 128, nk, VCH], F32, kind="ExternalInput").ap()
    wbt = nc.dram_tensor("wbt", [NOCH, 128, nk, OCH], F32, kind="ExternalInput").ap()
    at_d = nc.dram_tensor("at", [128, nk, 128], F32, kind="ExternalInput").ap()
    bt_d = nc.dram_tensor("bt", [128, OW], F32, kind="ExternalInput").ap()
    wrt = nc.dram_tensor("wrt", [128, nk, NE], F32, kind="ExternalInput").ap()
    bqp = nc.dram_tensor("bqp", [128, ncbl], F32, kind="ExternalInput").ap()
    bkp = nc.dram_tensor("bkp", [128, ncbl], F32, kind="ExternalInput").ap()
    bvr = nc.dram_tensor("bvr", [128, CW], F32, kind="ExternalInput").ap()
    bbr = nc.dram_tensor("bbr", [128, OW], F32, kind="ExternalInput").ap()
    bwt_d = nc.dram_tensor("bwt", [3, NHL], F32, kind="ExternalInput").ap()
    ibf_d = nc.dram_tensor("ibf", [128, 128], BF16, kind="ExternalInput").ap()
    if32_d = nc.dram_tensor("if32", [128, 128], F32, kind="ExternalInput").ap()
    rep_d = nc.dram_tensor("rep", [NB, 128], F32, kind="ExternalInput").ap()
    mask_d = nc.dram_tensor("mask", [128, 128], F32, kind="ExternalInput").ap()
    exp8_d = nc.dram_tensor("exp8", [NE, 128], F32, kind="ExternalInput").ap()
    row0_d = nc.dram_tensor("row0", [128, 128], BF16, kind="ExternalInput").ap()
    bkt3_d = nc.dram_tensor("bkt3", [3, NB * NB], F32, kind="ExternalInput").ap()
    revd_d = nc.dram_tensor("revd", [NHL, NB * NB], F32, kind="ExternalInput").ap()
    d1_d = nc.dram_tensor("d1", [NHL, NB * NB], F32, kind="ExternalInput").ap()
    ceps_d = nc.dram_tensor("ceps", [128, 1], F32, kind="ExternalInput").ap()
    cshift_d = nc.dram_tensor("cshift", [128, 1], F32, kind="ExternalInput").ap()

    y = nc.dram_tensor("y", [L, OW], F32, kind="ExternalOutput").ap()
    qts = nc.dram_tensor("qts", [ncbl, 128, L], BF16, kind="Internal").ap()
    kts = nc.dram_tensor("kts", [ncbl, 128, L], BF16, kind="Internal").ap()
    vs = nc.dram_tensor("vs", [ntb, 128, CW], BF16, kind="Internal").ap()
    xap = nc.dram_tensor("xap", [ntb, 128, CW], BF16, kind="Internal").ap()
    xag1 = nc.dram_tensor("xag1", [2 * (ntb // 2) * 128, CW], BF16, kind="Internal").ap()
    xag2 = nc.dram_tensor("xag2", [2 * (ntb // 2) * 128, CW], BF16, kind="Internal").ap()
    talld = nc.dram_tensor("talld", [NHL, NB * NB], F32, kind="Internal").ap()

    es = ExitStack()
    with es:
        def sb(name, shape, dt):
            return es.enter_context(nc.sbuf_tensor(name, shape, dt))

        ibf = sb("ibf_s", [128, 128], BF16)
        if32 = sb("if32_s", [128, 128], F32)
        rep_s = sb("rep_s", [NB, 128], F32)
        mask_s = sb("mask_s", [128, 128], F32)
        exp8_s = sb("exp8_s", [NE, 128], F32)
        row0_s = sb("row0_s", [128, 128], BF16)
        bkt3 = sb("bkt3_s", [3, NB * NB], F32)
        revd = sb("revd_s", [NHL, NB * NB], F32)
        d1s = sb("d1_s", [NHL, NB * NB], F32)
        bwt_s = sb("bwt_s", [3, NHL], F32)
        ceps_s = sb("ceps_s", [128, 1], F32)
        cshift_s = sb("cshift_s", [128, 1], F32)
        G_sb = sb("G_sb", [128, NHL * 128], BF16)
        bkp_s = sb("bkp_s", [128, ncbl], F32)
        bqbf = sb("bqbf", [128, ncbl], BF16)
        bqeff = sb("bqeff", [128, ncbl], F32)
        ttile = sb("ttile", [NB, NHL * NB], F32)
        tmp64 = sb("tmp64", [NHL, NB * NB], F32)
        tall = sb("tall", [NHL, NB * NB], F32)
        trow = sb("trow", [128, 2 * NB], F32)
        stats = sb("stats", [128, 6 * NBS], F32)
        mv = sb("mv", [128, 8], F32)
        stdv = sb("stdv", [128, 4], F32)
        rstd = sb("rstd", [128, 4], F32)
        recip = sb("recip", [128, QBC], F32)
        comb = sb("comb", [128, 2 * NE], F32)
        m1s = sb("m1s", [128, 8], F32)
        st4 = sb("st4", [128, 8], F32)
        w12 = sb("w12", [128, 4], F32)
        r2s = sb("r2s", [128, 2 * NE], F32)
        is1s = sb("is1s", [128, 2 * NE], F32)
        is2s = sb("is2s", [128, 2 * NE], F32)

        AB = 192 * 1024
        arena = sb("arena", [128, AB], U8)

        def carve(off, nbytes, dt):
            off = (off + 31) // 32 * 32
            assert off + nbytes <= AB, (off, nbytes)
            return arena[:, off:off + nbytes].bitcast(dt), off + nbytes

        xn1T_f, o = carve(0, nk * L * 2, BF16)
        xn1T = xn1T_f.rearrange("p (k t) -> p k t", k=nk)
        PH = (o + 31) // 32 * 32
        o = PH
        x_tile0, o = carve(o, D * 4, F32)
        x_tile1, o = carve(o, D * 4, F32)
        xn_tile0, o = carve(o, D * 2, BF16)
        xn_tile1, o = carve(o, D * 2, BF16)
        assert o <= AB
        x_tiles = (x_tile0, x_tile1)
        xn_tiles = (xn_tile0, xn_tile1)
        o = PH
        wq0, o = carve(o, nk * 128 * 2, BF16)
        wq1, o = carve(o, nk * 128 * 2, BF16)
        wv0, o = carve(o, nk * VCH * 2, BF16)
        wv1, o = carve(o, nk * VCH * 2, BF16)
        qraw, o = carve(o, 2 * TCQ * 2, BF16)
        stage, o = carve(o, 4 * 512 * 2, BF16)
        bvrbf, o = carve(o, CW * 2, BF16)
        assert o <= AB
        o = 0
        q_sb, o = carve(o, 2 * CBH * L * 2, BF16)
        k_sb, o = carve(o, 2 * CBH * L * 2, BF16)
        v_sb, o = carve(o, 2 * ntb * VW * 2, BF16)
        attnT, o = carve(o, 2 * ntb * TCQ * 2, BF16)
        xstg, o = carve(o, 4 * HCH * 2, BF16)
        PH3_TOP = o
        assert o <= AB
        o = 0
        xn2T_fa, o = carve(o, nk * LF * 2, BF16)
        xn2T_fb, o = carve(o, nk * LF * 2, BF16)
        xn2Ts = (xn2T_fa.rearrange("p (k t) -> p k t", k=nk),
                 xn2T_fb.rearrange("p (k t) -> p k t", k=nk))
        wb0, o = carve(o, nk * OCH * 2, BF16)
        wb1, o = carve(o, nk * OCH * 2, BF16)
        a_sb, o = carve(o, nk * 128 * 2, BF16)
        b_sb, o = carve(o, OW * 2, BF16)
        wr_sb, o = carve(o, nk * NE * 2, BF16)
        X4_OFF = (o + 31) // 32 * 32
        x4_t0, o = carve(o, D * 4, F32)
        x4_t1, o = carve(o, D * 4, F32)
        x4s = (x4_t0, x4_t1)
        xa4_t0, o = carve(o, D * 2, BF16)
        xa4_t1, o = carve(o, D * 2, BF16)
        xa4s = (xa4_t0, xa4_t1)
        xmid0, o = carve(o, D * 4, F32)
        xmid1, o = carve(o, D * 4, F32)
        xmids = (xmid0, xmid1)
        xn2_tile0, o = carve(o, D * 2, BF16)
        xn2_tile1, o = carve(o, D * 2, BF16)
        xn2_tiles = (xn2_tile0, xn2_tile1)
        hwT_a, o = carve(o, LF * 2, BF16)
        hwT_b, o = carve(o, LF * 2, BF16)
        hwTs = (hwT_a, hwT_b)
        combexp_a, o = carve(o, LF * 4, F32)
        combexp_b, o = carve(o, LF * 4, F32)
        combexps = (combexp_a, combexp_b)
        combT_a, o = carve(o, LF * 4, F32)
        combT_b, o = carve(o, LF * 4, F32)
        combTs = (combT_a, combT_b)
        bbrbf, o = carve(o, OW * 2, BF16)
        xa_sl, o = carve(o, 4 * OCH * 2, BF16)
        x_sl, o = carve(o, 4 * OCH * 4, F32)
        t1_st, o = carve(o, 2 * OCH * 4, F32)
        out_st, o = carve(o, 2 * OCH * 4, F32)
        assert o <= AB

        pb = [es.enter_context(nc.psum_tensor(f"pb{i}", [128, 512], F32))
              for i in range(8)]

        names = ("bar c_ld c_gl d0 t0 v0 m3 "
                 "x1le x1lo v1a s1q v1r v1n t1 v1e "
                 "w2le w2lo t2 v2a t2g e2 w2s0 w2s1 w2s2 w2s3 "
                 "q3le q3lo t3l s3e t3o v3r s3n w3s0 w3s1 w3s2 w3s3 ccs "
                 "x4le x4lo v4m s4a v4a s4q v4r v4n t4t v4e t4r v4c s4g t4c v4ct "
                 "t4x v4x t4h v4h g4a g4le g4lo t4f v4f s1e s4e "
                 "x4c0 x4c1 x4c2 x4c3 a4c0 a4c1 a4c2 a4c3 g4f y4s0 y4s1").split()
        S = {n: es.enter_context(nc.semaphore(n)) for n in names}
        block = es.enter_context(nc.Block())

        NC_LD = 14
        if NQC >= 2:
            _halves = [list(range(0, NQC // 2)), list(range(NQC // 2, NQC))]
        else:
            _halves = [list(range(NQC))]
        ATT_VH = []
        vhi = 0
        for hf in range(len(_halves)):
            for h in range(NHL):
                ATT_VH.append((vhi, h, _halves[hf]))
                vhi += 1
        NQCV = len(_halves[0])
        OGV = NQCV * QBC
        QI = ncbl * NQC
        KI = ncbl * NTCK
        VI = NVCHL * ntb
        E2T = QI + KI + VI
        LD3 = 2 * CBH + ntb
        OG = NQC * QBC
        TOG = NHL * OG
        CPT = 9
        FIP = NOCH * ntbf
        FI = NPASS * FIP
        CHTB = TCF // 128
        TT = NPASS * ntbf
        GS = []
        pend = []

        def _chunk_of(gd):
            if gd >= 0 and (gd % ntbf + 1) % CHTB == 0:
                return gd // ntbf, (gd % ntbf) // CHTB
            return None

        for g in range(TT):
            k = max(4, NOCH)
            while pend and k > 0:
                GS.append(("ffn",) + pend.pop(0))
                k -= 1
            GS.append(("chain", g, 0, 0))
            ck = _chunk_of(g - 2)
            if ck is not None:
                GS.append(("chunk", ck[0], ck[1], 0))
                for och in range(min(2, NOCH)):
                    for tb2 in range(ck[1] * CHTB, (ck[1] + 1) * CHTB):
                        pend.append((ck[0], och, tb2))
                if ck[1] == NQCF - 1:
                    for och in range(min(2, NOCH), NOCH):
                        for tb2 in range(ntbf):
                            pend.append((ck[0], och, tb2))
        GS.append(("flush", 0, 0, 0))
        for gd in range(TT - 2, TT):
            ck = _chunk_of(gd)
            if ck is not None:
                GS.append(("chunk", ck[0], ck[1], 0))
                for och in range(min(2, NOCH)):
                    for tb2 in range(ck[1] * CHTB, (ck[1] + 1) * CHTB):
                        pend.append((ck[0], och, tb2))
                if ck[1] == NQCF - 1:
                    for och in range(min(2, NOCH), NOCH):
                        for tb2 in range(ntbf):
                            pend.append((ck[0], och, tb2))
        while pend:
            GS.append(("ffn",) + pend.pop(0))
        FFN_ORDER = [(a, b, c) for kind, a, b, c in GS if kind == "ffn"]
        assert len(FFN_ORDER) == FI, (len(FFN_ORDER), FI)
        LASTIDX = {}
        LASTTB = {}
        LASTCH = {}
        for i, (ps_, och_, tb_) in enumerate(FFN_ORDER):
            LASTIDX[(ps_, och_)] = i
            LASTTB[(ps_, tb_)] = i
            LASTCH[(ps_, tb_ // CHTB)] = i

        @block.sync
        def _(s):
            for dst, src in ((ibf, ibf_d), (if32, if32_d), (rep_s, rep_d),
                             (mask_s, mask_d), (exp8_s, exp8_d),
                             (row0_s, row0_d), (bkt3, bkt3_d), (revd, revd_d),
                             (d1s, d1_d), (bwt_s, bwt_d), (bkp_s, bkp),
                             (ceps_s, ceps_d), (cshift_s, cshift_d)):
                s.dma_start(dst[:], src).then_inc(S["c_ld"], 16)
            s.dma_start(m1s[:, 3:4], ceps_d).then_inc(S["c_ld"], 16)
            s.wait_ge(S["v0"], 3)
            s.dma_start(talld, tall[:]).then_inc(S["d0"], 16)
            s.wait_ge(S["d0"], 16)
            s.dma_start(ttile[:].rearrange("i (h j) -> i h j", h=NHL),
                        talld.rearrange("h (i j) -> i h j", i=NB)
                        ).then_inc(S["d0"], 16)
            s.wait_ge(S["d0"], 32)
            s.sem_inc(S["bar"], 1)
            s.wait_ge(S["bar"], 5)
            for tb in range(ntb):
                if tb >= 2:
                    s.wait_ge(S["v1n"], tb - 1)
                s.dma_start(x_tiles[tb % 2], xc[tb * 128:(tb + 1) * 128, :]
                            ).then_inc(S["x1le" if tb % 2 == 0 else "x1lo"], 16)
            s.wait_ge(S["s1e"], ntb * NG)
            s.sem_inc(S["bar"], 1)
            s.wait_ge(S["bar"], 10)
            sidx = 0
            for cb in range(ncbl):
                for tc in range(NQC):
                    s.wait_ge(S["e2"], sidx + 1)
                    s.dma_start(qts[cb, :, tc * TCQ:(tc + 1) * TCQ],
                                stage[:, (sidx % 4) * 512:(sidx % 4) * 512 + TCQ]
                                ).then_inc(S[f"w2s{sidx % 4}"], 16)
                    sidx += 1
            for cb in range(ncbl):
                for tc in range(NTCK):
                    s.wait_ge(S["e2"], sidx + 1)
                    s.dma_start(kts[cb, :, tc * TCK:(tc + 1) * TCK],
                                stage[:, (sidx % 4) * 512:(sidx % 4) * 512 + TCK]
                                ).then_inc(S[f"w2s{sidx % 4}"], 16)
                    sidx += 1
            for ch in range(NVCHL):
                for tb in range(ntb):
                    s.wait_ge(S["e2"], sidx + 1)
                    s.dma_start(vs[tb, :, ch * VCH:(ch + 1) * VCH],
                                stage[:, (sidx % 4) * 512:(sidx % 4) * 512 + VCH]
                                ).then_inc(S[f"w2s{sidx % 4}"], 16)
                    sidx += 1
            for j in range(4):
                s.wait_ge(S[f"w2s{j}"], 16 * (E2T // 4 + (1 if E2T % 4 > j else 0)))
            s.sem_inc(S["bar"], 1)
            s.wait_ge(S["bar"], 15)
            s.wait_ge(S["t2"], E2T)
            for vh, h, qcs in ATT_VH:
                hb = vh % 2
                if vh >= 2:
                    s.wait_ge(S["t3o"], OGV * (vh - 1))
                q3 = "q3le" if vh % 2 == 0 else "q3lo"
                for cc in range(CBH):
                    s.dma_start(
                        q_sb[:, hb * CBH * L + cc * L:hb * CBH * L + (cc + 1) * L],
                        qts[h * CBH + cc]).then_inc(S[q3], 16)
                    s.dma_start(
                        k_sb[:, hb * CBH * L + cc * L:hb * CBH * L + (cc + 1) * L],
                        kts[h * CBH + cc]).then_inc(S[q3], 16)
                for mb in range(ntb):
                    s.dma_start(
                        v_sb[:, hb * ntb * VW + mb * VW:
                             hb * ntb * VW + mb * VW + HCH],
                        vs[mb, :, h * HCH:(h + 1) * HCH],
                    ).then_inc(S[q3], 16)
            s.sem_inc(S["bar"], 1)
            if X4_OFF < PH3_TOP:
                s.wait_ge(S["t3o"], TOG)
                for j in range(4):
                    s.wait_ge(S[f"w3s{j}"],
                              16 * (TOG // 4 + (1 if TOG % 4 > j else 0)))
            yfidx = 0
            seen_half2 = False
            s.wait_ge(S["ccs"], 1)
            for kind, a1, a2, a3 in GS:
                if kind == "chain":
                    gtb = a1
                    if gtb >= NTBH and not seen_half2:
                        s.wait_ge(S["ccs"], 2)
                        seen_half2 = True
                    xagN = xag1 if gtb < NTBH else xag2
                    htb = gtb % NTBH
                    g2 = gtb % 2
                    if gtb >= 2:
                        s.wait_ge(S["v4m"], gtb - 1)
                        s.wait_ge(S["s4a"], 2 * (gtb - 1))
                    x4s_sem = "x4le" if g2 == 0 else "x4lo"
                    s.dma_start(x4s[g2], xc[gtb * 128:(gtb + 1) * 128, :]
                                ).then_inc(S[x4s_sem], 16)
                    s.dma_start(xa4s[g2][:, 0:CW],
                                xagN[htb * 128:(htb + 1) * 128, :]
                                ).then_inc(S[x4s_sem], 16)
                    s.dma_start(xa4s[g2][:, CW:D],
                                xagN[(NTBH + htb) * 128:(NTBH + htb + 1) * 128, :]
                                ).then_inc(S[x4s_sem], 16)
                elif kind == "ffn":
                    p, och, tb = a1, a2, a3
                    gtb = p * ntbf + tb
                    if yfidx >= 4:
                        s.wait_ge(S["v4f"], yfidx - 3)
                        s.wait_ge(S["g4f"], yfidx - 3)
                    s.dma_start(
                        xa_sl[:, (yfidx % 4) * OCH:(yfidx % 4 + 1) * OCH],
                        xap[gtb, :, och * OCH:(och + 1) * OCH]
                    ).then_inc(S[f"a4c{yfidx % 4}"], 16)
                    s.dma_start(
                        x_sl[:, (yfidx % 4) * OCH:(yfidx % 4 + 1) * OCH],
                        xres[gtb * 128:(gtb + 1) * 128, och * OCH:(och + 1) * OCH]
                    ).then_inc(S[f"x4c{yfidx % 4}"], 16)
                    yfidx += 1

        @block.gpsimd
        def _(g):
            g.dma_start(bqbf[:], bqp).then_inc(S["c_gl"], 16)
            g.dma_start(bvrbf[:], bvr).then_inc(S["c_gl"], 16)
            g.sem_inc(S["bar"], 1)
            g.wait_ge(S["bar"], 5)
            g.sem_inc(S["bar"], 1)
            g.wait_ge(S["bar"], 10)
            g.wait_ge(S["v1n"], ntb)
            g.wait_ge(S["t1"], ntb * NG)
            for cb in range(ncbl):
                if cb >= 2:
                    g.wait_ge(S["t2"], NQC * (cb - 1))
                g.dma_start((wq0 if cb % 2 == 0 else wq1)[:],
                            wqt[cb].rearrange("p k c -> p (k c)")
                            ).then_inc(S["w2le" if cb % 2 == 0 else "w2lo"], 16)
            for cb in range(ncbl):
                if cb == 0:
                    g.wait_ge(S["t2"], (ncbl - 1) * NQC)
                elif cb == 1:
                    g.wait_ge(S["t2"], QI)
                else:
                    g.wait_ge(S["t2"], QI + NTCK * (cb - 1))
                g.dma_start((wq0 if cb % 2 == 0 else wq1)[:],
                            wkt[cb].rearrange("p k c -> p (k c)")
                            ).then_inc(S["w2le" if cb % 2 == 0 else "w2lo"], 16)
            for ch in range(NVCHL):
                if ch == 0:
                    g.wait_ge(S["w2le"], 16 * ncbl)
                elif ch == 1:
                    g.wait_ge(S["w2lo"], 16 * ncbl)
                else:
                    g.wait_ge(S["t2"], QI + KI + ntb * (ch - 1))
                g.dma_start((wv0 if ch % 2 == 0 else wv1)[:],
                            wvt[ch].rearrange("p k c -> p (k c)")
                            ).then_inc(S["w2le" if ch % 2 == 0 else "w2lo"], 16)
            g.sem_inc(S["bar"], 1)
            g.wait_ge(S["bar"], 15)
            g.sem_inc(S["bar"], 1)
            slot_seq = [[] for _ in range(4)]
            _qcg = 0
            for vh, h, qcs in ATT_VH:
                for qc in qcs:
                    for qb in range(QBC):
                        og = _qcg * QBC + qb
                        slot_seq[og % 4].append(qc * QBC + qb)
                    _qcg += 1
            for j in range(4):
                fh = [i for i, tb_ in enumerate(slot_seq[j]) if tb_ < NTBH]
                thr = (fh[-1] + 1) if fh else 0
                if thr:
                    g.wait_ge(S[f"w3s{j}"], 16 * thr)
            g.collective_compute(
                "AllGather", mybir.AluOpType.bypass,
                replica_groups=[[0, 1], [2, 3], [4, 5], [6, 7]],
                ins=[xap[0:NTBH].rearrange("m p c -> (m p) c").opt()],
                outs=[xag1.opt()],
            ).then_inc(S["ccs"], 1)
            g.wait_ge(S["t3o"], TOG)
            for j in range(4):
                g.wait_ge(S[f"w3s{j}"], 16 * len(slot_seq[j]))
            g.collective_compute(
                "AllGather", mybir.AluOpType.bypass,
                replica_groups=[[0, 1], [2, 3], [4, 5], [6, 7]],
                ins=[xap[NTBH:ntb].rearrange("m p c -> (m p) c").opt()],
                outs=[xag2.opt()],
            ).then_inc(S["ccs"], 1)
            g.dma_start(a_sb[:], at_d.rearrange("p k c -> p (k c)")).then_inc(S["g4a"], 16)
            g.dma_start(b_sb[:], bt_d).then_inc(S["g4a"], 16)
            g.dma_start(wr_sb[:], wrt.rearrange("p k c -> p (k c)")).then_inc(S["g4a"], 16)
            g.dma_start(bbrbf[:], bbr).then_inc(S["g4a"], 16)
            WLOADS = [(p_, och_) for p_ in range(NPASS) for och_ in range(NOCH)]
            WPREV = {}
            for j in range(len(WLOADS)):
                for j2 in range(j - 1, -1, -1):
                    if WLOADS[j2][1] % 2 == WLOADS[j][1] % 2:
                        WPREV[j] = j2
                        break
            n_up = 0
            while n_up < len(WLOADS) and n_up not in WPREV:
                n_up += 1
            trig = {}
            for j in range(n_up, len(WLOADS)):
                trig.setdefault(LASTIDX[WLOADS[WPREV[j]]], []).append(j)

            def wload(j):
                p_, och_ = WLOADS[j]
                if j in WPREV:
                    g.wait_ge(S["t4f"], LASTIDX[WLOADS[WPREV[j]]] + 1)
                g.dma_start((wb0 if och_ % 2 == 0 else wb1)[:],
                            wbt[och_].rearrange("p k c -> p (k c)")
                            ).then_inc(S["g4le" if och_ % 2 == 0 else "g4lo"], 16)

            for j in range(n_up):
                wload(j)
            for gf, (p, och, tb) in enumerate(FFN_ORDER):
                g.wait_ge(S["v4f"], gf + 1)
                g.wait_ge(S[f"x4c{gf % 4}"], 16 * (gf // 4 + 1))
                if gf >= 2:
                    g.wait_ge(S[f"y4s{gf % 2}"], 16 * (gf // 2))
                g.tensor_add(
                    out_st[:, (gf % 2) * OCH:(gf % 2 + 1) * OCH],
                    t1_st[:, (gf % 2) * OCH:(gf % 2 + 1) * OCH],
                    x_sl[:, (gf % 4) * OCH:(gf % 4 + 1) * OCH],
                ).then_inc(S["g4f"], 1)
                for j in trig.get(gf, []):
                    wload(j)

        @block.tensor
        def _(t):
            t.wait_ge(S["c_ld"], 16 * NC_LD)
            t.wait_ge(S["c_gl"], 32)
            t.matmul(pb[6][0:NHL, 0:NB * NB], bwt_s[:, 0:NHL], bkt3[:, :],
                     start=True, stop=True).then_inc(S["t0"], 1)
            t.wait_ge(S["d0"], 32)
            for h in range(NHL):
                t.wait_ge(S["v0"], 1 if h == 0 else 3 + 2 * (h - 1) + 1)
                t.matmul(pb[6][:, 256 + h * NB:256 + (h + 1) * NB], rep_s[:, :],
                         ttile[:, h * NB:(h + 1) * NB],
                         start=True, stop=True).then_inc(S["t0"], 1)
            t.wait_ge(S["v0"], 3 + 2 * NHL)
            for cb in range(ncbl):
                h = cb // CBH
                mm = t.matmul(pb[7][:, cb:cb + 1],
                              G_sb[:, h * 128:(h + 1) * 128],
                              bqbf[:, cb:cb + 1], start=True, stop=True)
            mm.then_inc(S["t0"], 1)
            t.sem_inc(S["bar"], 1)
            t.wait_ge(S["bar"], 5)
            gidx = 0
            for tb in range(ntb):
                t.wait_ge(S["v1n"], tb + 1)
                for gp in range(NG):
                    if gidx >= 2:
                        t.wait_ge(S["s1e"], gidx - 1)
                    for j in range(4):
                        k = gp * 4 + j
                        mm = t.matmul(pb[gidx % 2][:, j * 128:(j + 1) * 128],
                                      xn_tiles[tb % 2][:, k * 128:(k + 1) * 128],
                                      ibf[:], start=True, stop=True)
                    mm.then_inc(S["t1"], 1)
                    gidx += 1
            t.sem_inc(S["bar"], 1)
            t.wait_ge(S["bar"], 10)
            idx = 0
            for cb in range(ncbl):
                wbuf = (wq0 if cb % 2 == 0 else wq1)
                h = cb // CBH
                t.wait_ge(S["w2le" if cb % 2 == 0 else "w2lo"], 16 * (cb // 2 + 1))
                for tc in range(NQC):
                    if idx >= 2:
                        t.wait_ge(S["v2a"], idx - 1)
                    for k in range(nk):
                        mm = t.matmul(pb[idx % 2][:, 0:TCQ],
                                      wbuf[:, k * 128:(k + 1) * 128],
                                      xn1T[:, k, tc * TCQ:(tc + 1) * TCQ],
                                      start=(k == 0), stop=(k == nk - 1))
                    mm.then_inc(S["t2"], 1)
                    t.wait_ge(S["v2a"], idx + 1)
                    if idx >= 2:
                        t.wait_ge(S["e2"], idx - 1)
                    t.matmul(pb[2 + idx % 2][:, 0:TCQ],
                             G_sb[:, h * 128:(h + 1) * 128],
                             qraw[:, (idx % 2) * TCQ:(idx % 2 + 1) * TCQ],
                             start=True, stop=True).then_inc(S["t2g"], 1)
                    idx += 1
            for cb in range(ncbl):
                wbuf = (wq0 if cb % 2 == 0 else wq1)
                t.wait_ge(S["w2le" if cb % 2 == 0 else "w2lo"],
                          16 * (ncbl // 2 + cb // 2 + 1))
                for tc in range(NTCK):
                    if idx >= 2:
                        t.wait_ge(S["e2"], idx - 1)
                    for k in range(nk):
                        mm = t.matmul(pb[idx % 2][:, 0:TCK],
                                      wbuf[:, k * 128:(k + 1) * 128],
                                      xn1T[:, k, tc * TCK:(tc + 1) * TCK],
                                      start=(k == 0), stop=(k == nk - 1))
                    mm.then_inc(S["t2"], 1)
                    idx += 1
            for ch in range(NVCHL):
                vbuf = (wv0 if ch % 2 == 0 else wv1)
                t.wait_ge(S["w2le" if ch % 2 == 0 else "w2lo"],
                          16 * (ncbl + ch // 2 + 1))
                for tb in range(ntb):
                    if idx >= 2:
                        t.wait_ge(S["e2"], idx - 1)
                    for k in range(nk):
                        t.matmul(pb[idx % 2][:, 0:VCH],
                                 xn1T[:, k, tb * 128:(tb + 1) * 128],
                                 vbuf[:, k * VCH:(k + 1) * VCH],
                                 start=(k == 0), stop=False)
                    t.matmul(pb[idx % 2][:, 0:VCH], row0_s[:],
                             bvrbf[:, ch * VCH:(ch + 1) * VCH],
                             start=False, stop=True).then_inc(S["t2"], 1)
                    idx += 1
            t.sem_inc(S["bar"], 1)
            t.wait_ge(S["bar"], 15)
            t.wait_ge(S["m3"], 2)
            qcg = -1
            for vh, h, qcs in ATT_VH:
                hb = vh % 2
                t.wait_ge(S["q3le" if vh % 2 == 0 else "q3lo"],
                          16 * LD3 * (vh // 2 + 1))
                for qc in qcs:
                    qcg += 1
                    for mb in range(ntb):
                        li = qcg * ntb + mb
                        if li >= 4:
                            t.wait_ge(S["s3e"], li - 3)
                        for cc in range(CBH):
                            mm = t.matmul(
                                pb[(0, 1, 6, 7)[li % 4]][:, 0:TCQ],
                                k_sb[:, hb * CBH * L + cc * L + mb * 128:
                                     hb * CBH * L + cc * L + mb * 128 + 128],
                                q_sb[:, hb * CBH * L + cc * L + qc * TCQ:
                                     hb * CBH * L + cc * L + qc * TCQ + TCQ],
                                start=(cc == 0), stop=(cc == CBH - 1))
                        mm.then_inc(S["t3l"], 1)
                    t.wait_ge(S["s3e"], ntb * (qcg + 1))
                    for qb in range(QBC):
                        if qcg >= 1:
                            t.wait_ge(S["s3n"], (qcg - 1) * QBC + qb + 1)
                        for mb in range(ntb):
                            mm = t.matmul(
                                pb[2 + qb][:, 0:VW],
                                attnT[:, (qcg % 2) * ntb * TCQ + mb * TCQ + qb * 128:
                                      (qcg % 2) * ntb * TCQ + mb * TCQ + qb * 128 + 128],
                                v_sb[:, hb * ntb * VW + mb * VW:
                                     hb * ntb * VW + (mb + 1) * VW],
                                start=(mb == 0), stop=(mb == ntb - 1))
                        mm.then_inc(S["t3o"], 1)
            t.sem_inc(S["bar"], 1)
            t.wait_ge(S["bar"], 20)
            gidx = [0]

            def t_chain(gtb):
                t.wait_ge(S["v4n"], gtb + 1)
                for gp in range(NG):
                    if gidx[0] >= 2:
                        t.wait_ge(S["s4e"], gidx[0] - 1)
                    for j in range(4):
                        k = gp * 4 + j
                        mm = t.matmul(pb[6 + gidx[0] % 2][:, j * 128:(j + 1) * 128],
                                      xn2_tiles[gtb % 2][:, k * 128:(k + 1) * 128],
                                      ibf[:], start=True, stop=True)
                    mm.then_inc(S["t4t"], 1)
                    gidx[0] += 1
                t.wait_ge(S["s4e"], (gtb + 1) * NG)
                t.wait_ge(S["g4a"], 16 * 4)
                if gtb >= 1:
                    t.wait_ge(S["v4c"], (gtb - 1) * CPT + 3)
                tb = gtb % ntbf
                pp = (gtb // ntbf) % 2
                for k in range(nk):
                    mm = t.matmul(pb[2][:, 0:NE],
                                  xn2Ts[pp][:, k, tb * 128:(tb + 1) * 128],
                                  wr_sb[:, k * NE:(k + 1) * NE],
                                  start=(k == 0), stop=(k == nk - 1))
                mm.then_inc(S["t4r"], 1)

            def t_combT(gtb):
                t.wait_ge(S["v4c"], gtb * CPT + CPT)
                if gtb >= 1:
                    t.wait_ge(S["v4ct"], gtb)
                tb = gtb % ntbf
                pp = (gtb // ntbf) % 2
                t.matmul(pb[3][0:NE, 0:128],
                         comb[:, (gtb % 2) * NE:(gtb % 2 + 1) * NE],
                         if32[:], start=True, stop=True).then_inc(S["t4c"], 1)

            fidx = 0
            for kind, a1, a2, a3 in GS:
                if kind == "chain":
                    g = a1
                    if g >= 1:
                        t_chain(g - 1)
                    if g >= 2:
                        t_combT(g - 2)
                elif kind == "flush":
                    t_chain(TT - 1)
                    t_combT(TT - 2)
                    t_combT(TT - 1)
                elif kind == "chunk":
                    p, c = a1, a2
                    ce = p * NQCF + c
                    t.wait_ge(S["v4ct"], p * ntbf + (c + 1) * CHTB)
                    if ce >= 1:
                        t.wait_ge(S["v4x"], ce)
                    t.matmul(pb[4][:, 0:TCF], exp8_s[:, :],
                             combTs[p % 2][0:NE, c * TCF:(c + 1) * TCF],
                             start=True, stop=True).then_inc(S["t4x"], 1)
                    if ce >= 1:
                        t.wait_ge(S["v4h"], ce)
                    for k in range(nk):
                        mm = t.matmul(pb[5][:, 0:TCF],
                                      a_sb[:, k * 128:(k + 1) * 128],
                                      xn2Ts[p % 2][:, k, c * TCF:(c + 1) * TCF],
                                      start=(k == 0), stop=(k == nk - 1))
                    mm.then_inc(S["t4h"], 1)
                else:
                    p, och, tb = a1, a2, a3
                    if och % 2 == 0:
                        t.wait_ge(S["g4le"], 16 * (p * ((NOCH + 1) // 2) + och // 2 + 1))
                    else:
                        t.wait_ge(S["g4lo"], 16 * (p * (NOCH // 2) + och // 2 + 1))
                    t.wait_ge(S["s4e"], (p * ntbf + tb + 1) * NG)
                    t.wait_ge(S["v4h"], p * NQCF + tb // CHTB + 1)
                    if fidx >= 2:
                        t.wait_ge(S["v4f"], fidx - 1)
                    wbuf = (wb0 if och % 2 == 0 else wb1)
                    for k in range(nk):
                        t.matmul(pb[fidx % 2][:, 0:OCH],
                                 xn2Ts[p % 2][:, k, tb * 128:(tb + 1) * 128],
                                 wbuf[:, k * OCH:(k + 1) * OCH],
                                 start=(k == 0), stop=False)
                    t.matmul(pb[fidx % 2][:, 0:OCH],
                             hwTs[p % 2][:, tb * 128:(tb + 1) * 128],
                             b_sb[:, och * OCH:(och + 1) * OCH],
                             start=False, stop=False)
                    t.matmul(pb[fidx % 2][:, 0:OCH], row0_s[:],
                             bbrbf[:, och * OCH:(och + 1) * OCH],
                             start=False, stop=True).then_inc(S["t4f"], 1)
                    fidx += 1

        @block.vector
        def _(v):
            v.wait_ge(S["t0"], 1)
            v.tensor_tensor(tmp64[:], pb[6][0:NHL, 0:NB * NB], revd[:],
                            op=ALU.mult).then_inc(S["v0"], 1)
            v.wait_ge(S["v0"], 1)
            v.tensor_tensor(tall[:], tmp64[:], d1s[:], op=ALU.add
                            ).then_inc(S["v0"], 1)
            v.sem_inc(S["v0"], 1)
            for h in range(NHL):
                v.wait_ge(S["t0"], 2 + h)
                v.tensor_copy(trow[:, (h % 2) * NB:(h % 2 + 1) * NB],
                              pb[6][:, 256 + h * NB:256 + (h + 1) * NB]
                              ).then_inc(S["v0"], 1)
                v.wait_ge(S["v0"], 3 + 2 * h + 1)
                for rp in range(128 // NB):
                    mm = v.tensor_tensor(
                        G_sb[:, h * 128 + rp * NB:h * 128 + (rp + 1) * NB],
                        trow[:, (h % 2) * NB:(h % 2 + 1) * NB],
                        mask_s[:, rp * NB:(rp + 1) * NB], op=ALU.mult)
                mm.then_inc(S["v0"], 1)
            v.wait_ge(S["t0"], 2 + NHL)
            v.tensor_copy(bqeff[:, 0:ncbl], pb[7][:, 0:ncbl])
            v.sem_inc(S["bar"], 1)
            v.wait_ge(S["bar"], 5)
            for tb in range(ntb):
                v.wait_ge(S["x1le" if tb % 2 == 0 else "x1lo"], 16 * (tb // 2 + 1))
                for j in range(NBS):
                    bs = v.bn_stats(stats[:, j * 6:(j + 1) * 6],
                                    x_tiles[tb % 2][:, j * 512:(j + 1) * 512])
                bs.then_inc(S["v1a"], 1)
                v.wait_ge(S["v1a"], 2 * tb + 1)
                v.bn_aggr(mv[:, 0:2], stats[:, 0:6 * NBS]).then_inc(S["v1a"], 1)
                v.wait_ge(S["s1q"], tb + 1)
                v.reciprocal(rstd[:, 0:1], stdv[:, 0:1]).then_inc(S["v1r"], 1)
                v.wait_ge(S["v1r"], tb + 1)
                if tb >= 2:
                    v.wait_ge(S["t1"], (tb - 1) * NG)
                v.tensor_scalar(xn_tiles[tb % 2][:], x_tiles[tb % 2][:],
                                mv[:, 0:1], rstd[:, 0:1],
                                op0=ALU.subtract, op1=ALU.mult
                                ).then_inc(S["v1n"], 1)
            v.sem_inc(S["bar"], 1)
            v.wait_ge(S["bar"], 10)
            idx = 0
            sidx = 0
            for cb in range(ncbl):
                for tc in range(NQC):
                    v.wait_ge(S["t2"], idx + 1)
                    if idx >= 2:
                        v.wait_ge(S["t2g"], idx - 1)
                    v.tensor_copy(qraw[:, (idx % 2) * TCQ:(idx % 2 + 1) * TCQ],
                                  pb[idx % 2][:, 0:TCQ]).then_inc(S["v2a"], 1)
                    v.wait_ge(S["t2g"], idx + 1)
                    if sidx >= 4:
                        v.wait_ge(S[f"w2s{sidx % 4}"], 16 * (sidx // 4))
                    v.tensor_scalar(
                        stage[:, (sidx % 4) * 512:(sidx % 4) * 512 + TCQ],
                        pb[2 + idx % 2][:, 0:TCQ], bqeff[:, cb:cb + 1], None,
                        op0=ALU.add).then_inc(S["e2"], 1)
                    idx += 1
                    sidx += 1
            for cb in range(ncbl):
                for tc in range(NTCK):
                    v.wait_ge(S["t2"], idx + 1)
                    if sidx >= 4:
                        v.wait_ge(S[f"w2s{sidx % 4}"], 16 * (sidx // 4))
                    v.tensor_scalar(
                        stage[:, (sidx % 4) * 512:(sidx % 4) * 512 + TCK],
                        pb[idx % 2][:, 0:TCK], bkp_s[:, cb:cb + 1], None,
                        op0=ALU.add).then_inc(S["e2"], 1)
                    idx += 1
                    sidx += 1
            for ch in range(NVCHL):
                for tb in range(ntb):
                    v.wait_ge(S["t2"], idx + 1)
                    if sidx >= 4:
                        v.wait_ge(S[f"w2s{sidx % 4}"], 16 * (sidx // 4))
                    v.tensor_copy(
                        stage[:, (sidx % 4) * 512:(sidx % 4) * 512 + VCH],
                        pb[idx % 2][:, 0:VCH]).then_inc(S["e2"], 1)
                    idx += 1
                    sidx += 1
            v.sem_inc(S["bar"], 1)
            v.wait_ge(S["bar"], 15)
            v.wait_ge(S["t2"], E2T)
            for hb in range(2):
                vv = v_sb[:, hb * ntb * VW:(hb + 1) * ntb * VW] \
                    .rearrange("p (m c) -> p m c", m=ntb)
                v.memset(vv[:, :, HCH:HCH + 1], 1.0).then_inc(S["m3"], 1)
            qcg = -1
            for vh, h, qcs in ATT_VH:
                for qc in qcs:
                    qcg += 1
                    for qb in range(QBC):
                        og = qcg * QBC + qb
                        v.wait_ge(S["t3o"], og + 1)
                        if qcg >= 1:
                            v.wait_ge(S["s3n"], (qcg - 1) * QBC + qb + 1)
                        v.reciprocal(recip[:, qb:qb + 1],
                                     pb[2 + qb][:, HCH:HCH + 1]
                                     ).then_inc(S["v3r"], 1)
            v.sem_inc(S["bar"], 1)
            def v_s1(gtb):
                g2 = gtb % 2
                v.wait_ge(S["x4le" if g2 == 0 else "x4lo"], 48 * (gtb // 2 + 1))
                if gtb >= 2:
                    v.wait_ge(S["v4n"], gtb - 1)
                    v.wait_ge(S["s4a"], 2 * (gtb - 1))
                v.tensor_tensor(xmids[g2][:], x4s[g2][:], xa4s[g2][:],
                                op=ALU.add).then_inc(S["v4m"], 1)

            def v_s23(gtb):
                g2 = gtb % 2
                v.wait_ge(S["s4a"], 2 * (gtb + 1))
                v.tensor_scalar(mv[:, 4 + 2 * g2:5 + 2 * g2],
                                st4[:, 4 * g2:4 * g2 + 1], 1.0 / D, None,
                                op0=ALU.mult).then_inc(S["v4a"], 1)
                v.wait_ge(S["v4a"], 3 * gtb + 1)
                v.tensor_scalar(st4[:, 4 * g2 + 2:4 * g2 + 3],
                                mv[:, 4 + 2 * g2:5 + 2 * g2],
                                mv[:, 4 + 2 * g2:5 + 2 * g2], None,
                                op0=ALU.mult).then_inc(S["v4a"], 1)
                v.wait_ge(S["v4a"], 3 * gtb + 2)
                v.scalar_tensor_tensor(mv[:, 5 + 2 * g2:6 + 2 * g2],
                                       st4[:, 4 * g2 + 1:4 * g2 + 2], 1.0 / D,
                                       st4[:, 4 * g2 + 2:4 * g2 + 3],
                                       op0=ALU.mult,
                                       op1=ALU.subtract).then_inc(S["v4a"], 1)
                v.wait_ge(S["s4q"], gtb + 1)
                v.reciprocal(rstd[:, 2 + g2:3 + g2], stdv[:, 2 + g2:3 + g2]
                             ).then_inc(S["v4r"], 1)
                v.wait_ge(S["v4r"], gtb + 1)
                if gtb >= 2:
                    v.wait_ge(S["t4t"], (gtb - 1) * NG)
                v.tensor_scalar(xn2_tiles[g2][:], xmids[g2][:],
                                mv[:, 4 + 2 * g2:5 + 2 * g2],
                                rstd[:, 2 + g2:3 + g2],
                                op0=ALU.subtract, op1=ALU.mult
                                ).then_inc(S["v4n"], 1)

            def v_s4(gtb):
                g2 = gtb % 2
                tb = gtb % ntbf
                p_ = gtb // ntbf
                if p_ >= 2:
                    v.wait_ge(S["t4x"], (p_ - 2) * NQCF + tb // CHTB + 1)
                v.wait_ge(S["t4r"], gtb + 1)
                if gtb >= 1:
                    v.wait_ge(S["s4g"], gtb - 1)
                cb_ = gtb * CPT
                sA = slice(g2 * NE, (g2 + 1) * NE)
                v.reduce_max(m1s[:, 4 * g2:4 * g2 + 1], pb[2][:, 0:NE], axis=AX.X
                             ).then_inc(S["v4c"], 1)
                v.wait_ge(S["v4c"], cb_ + 1)
                v.tensor_scalar(is1s[:, sA], pb[2][:, 0:NE],
                                m1s[:, 4 * g2:4 * g2 + 1], None,
                                op0=ALU.is_ge).then_inc(S["v4c"], 1)
                v.wait_ge(S["v4c"], cb_ + 2)
                v.scalar_tensor_tensor(r2s[:, sA], is1s[:, sA], -1e30,
                                       pb[2][:, 0:NE],
                                       op0=ALU.mult, op1=ALU.add
                                       ).then_inc(S["v4c"], 1)
                v.wait_ge(S["v4c"], cb_ + 3)
                v.reduce_max(m1s[:, 4 * g2 + 1:4 * g2 + 2], r2s[:, sA], axis=AX.X
                             ).then_inc(S["v4c"], 1)
                v.wait_ge(S["v4c"], cb_ + 4)
                v.tensor_tensor(m1s[:, 4 * g2 + 2:4 * g2 + 3],
                                m1s[:, 4 * g2:4 * g2 + 1],
                                m1s[:, 4 * g2 + 1:4 * g2 + 2],
                                op=ALU.subtract).then_inc(S["v4c"], 1)
                v.wait_ge(S["v4c"], cb_ + 5)
                v.tensor_scalar(is2s[:, sA], r2s[:, sA],
                                m1s[:, 4 * g2 + 1:4 * g2 + 2], None,
                                op0=ALU.is_ge).then_inc(S["v4c"], 1)
                v.wait_ge(S["s4g"], gtb + 1)
                v.tensor_scalar(w12[:, 2 * g2 + 1:2 * g2 + 2],
                                w12[:, 2 * g2:2 * g2 + 1], -1.0, 1.0,
                                op0=ALU.mult, op1=ALU.add).then_inc(S["v4c"], 1)
                v.wait_ge(S["v4c"], cb_ + 7)
                v.tensor_scalar(is1s[:, sA], is1s[:, sA],
                                w12[:, 2 * g2:2 * g2 + 1], None,
                                op0=ALU.mult).then_inc(S["v4c"], 1)
                v.wait_ge(S["v4c"], cb_ + 8)
                if gtb >= 1:
                    v.wait_ge(S["t4c"], gtb)
                v.scalar_tensor_tensor(comb[:, g2 * NE:(g2 + 1) * NE],
                                       is2s[:, sA],
                                       w12[:, 2 * g2 + 1:2 * g2 + 2],
                                       is1s[:, sA],
                                       op0=ALU.mult, op1=ALU.add
                                       ).then_inc(S["v4c"], 1)
                v.wait_ge(S["t4c"], gtb + 1)
                v.tensor_copy(combTs[p_ % 2][0:NE, tb * 128:(tb + 1) * 128],
                              pb[3][0:NE, 0:128]).then_inc(S["v4ct"], 1)

            vfidx = 0
            for kind, a1, a2, a3 in GS:
                if kind == "chain":
                    g = a1
                    v_s1(g)
                    if g >= 1:
                        v_s23(g - 1)
                    if g >= 2:
                        v_s4(g - 2)
                elif kind == "flush":
                    v_s23(TT - 1)
                    v_s4(TT - 2)
                    v_s4(TT - 1)
                elif kind == "chunk":
                    p, c = a1, a2
                    ce = p * NQCF + c
                    v.wait_ge(S["t4x"], ce + 1)
                    v.tensor_copy(combexps[p % 2][:, c * TCF:(c + 1) * TCF],
                                  pb[4][:, 0:TCF]).then_inc(S["v4x"], 1)
                    v.wait_ge(S["t4h"], ce + 1)
                    v.wait_ge(S["v4x"], ce + 1)
                    if p >= 2 and (p - 2, c) in LASTCH:
                        v.wait_ge(S["t4f"], LASTCH[(p - 2, c)] + 1)
                    v.tensor_tensor(hwTs[p % 2][:, c * TCF:(c + 1) * TCF],
                                    pb[5][:, 0:TCF],
                                    combexps[p % 2][:, c * TCF:(c + 1) * TCF],
                                    op=ALU.mult).then_inc(S["v4h"], 1)
                else:
                    p, och, tb = a1, a2, a3
                    v.wait_ge(S["t4f"], vfidx + 1)
                    v.wait_ge(S[f"a4c{vfidx % 4}"], 16 * (vfidx // 4 + 1))
                    if vfidx >= 2:
                        v.wait_ge(S["g4f"], vfidx - 1)
                    v.tensor_tensor(t1_st[:, (vfidx % 2) * OCH:(vfidx % 2 + 1) * OCH],
                                    pb[vfidx % 2][:, 0:OCH],
                                    xa_sl[:, (vfidx % 4) * OCH:(vfidx % 4 + 1) * OCH],
                                    op=ALU.add).then_inc(S["v4f"], 1)
                    vfidx += 1

        @block.scalar
        def _(sc):
            sc.sem_inc(S["bar"], 1)
            sc.wait_ge(S["bar"], 5)
            for tb in range(ntb):
                sc.wait_ge(S["v1a"], 2 * (tb + 1))
                sc.activation(stdv[:, 0:1], mv[:, 1:2], AF.Sqrt, bias=ceps_s[:, 0:1]
                              ).then_inc(S["s1q"], 1)
                for gp in range(NG):
                    gi = tb * NG + gp
                    sc.wait_ge(S["t1"], gi + 1)
                    sc.activation(
                        xn1T[:, gp * 4:gp * 4 + 4, tb * 128:(tb + 1) * 128],
                        pb[gi % 2][:].rearrange("p (a b) -> p a b", a=4),
                        AF.Copy).then_inc(S["s1e"], 1)
            sc.sem_inc(S["bar"], 1)
            sc.wait_ge(S["bar"], 10)
            sc.sem_inc(S["bar"], 1)
            sc.wait_ge(S["bar"], 15)
            qcg = -1
            for vh, h, qcs in ATT_VH:
                for qc in qcs:
                    qcg += 1
                    for mb in range(ntb):
                        li = qcg * ntb + mb
                        sc.wait_ge(S["t3l"], li + 1)
                        if qcg >= 2:
                            sc.wait_ge(S["t3o"], QBC * (qcg - 1))
                        sc.activation(
                            attnT[:, (qcg % 2) * ntb * TCQ + mb * TCQ:
                                  (qcg % 2) * ntb * TCQ + (mb + 1) * TCQ],
                            pb[(0, 1, 6, 7)[li % 4]][:, 0:TCQ], AF.Exp,
                            bias=cshift_s[:, 0:1]).then_inc(S["s3e"], 1)
                    for qb in range(QBC):
                        og = qcg * QBC + qb
                        sc.wait_ge(S["v3r"], og + 1)
                        sl = og % 4
                        if og >= 4:
                            sc.wait_ge(S[f"w3s{sl}"], 16 * (og // 4))
                        sc.activation(
                            xstg[:, sl * HCH:(sl + 1) * HCH],
                            pb[2 + qb][:, 0:HCH], AF.Copy,
                            scale=recip[:, qb:qb + 1]).then_inc(S["s3n"], 1)
                        sc.wait_ge(S["s3n"], og + 1)
                        tb = qc * QBC + qb
                        sc.dma_start(
                            xap[tb, :, h * HCH:(h + 1) * HCH],
                            xstg[:, sl * HCH:(sl + 1) * HCH]
                        ).then_inc(S[f"w3s{sl}"], 16)
            sc.sem_inc(S["bar"], 1)
            for j in range(4):
                sc.wait_ge(S[f"w3s{j}"],
                           16 * (TOG // 4 + (1 if TOG % 4 > j else 0)))

            def s_stats(gtb):
                g2 = gtb % 2
                sc.wait_ge(S["v4m"], gtb + 1)
                if gtb >= 2:
                    sc.wait_ge(S["v4a"], 3 * (gtb - 1))
                sc.activation(xa4s[g2][:], xmids[g2][:], AF.Copy,
                              accum_out=st4[:, 4 * g2:4 * g2 + 1]
                              ).then_inc(S["s4a"], 1)
                sc.wait_ge(S["s4a"], 2 * gtb + 1)
                sc.activation(xa4s[g2][:], xmids[g2][:], AF.Square,
                              accum_out=st4[:, 4 * g2 + 1:4 * g2 + 2]
                              ).then_inc(S["s4a"], 1)

            def s_sqrt(gtb):
                g2 = gtb % 2
                sc.wait_ge(S["v4a"], 3 * (gtb + 1))
                if gtb >= 2:
                    sc.wait_ge(S["v4r"], gtb - 1)
                sc.activation(stdv[:, 2 + g2:3 + g2], mv[:, 5 + 2 * g2:6 + 2 * g2],
                              AF.Sqrt, bias=ceps_s[:, 0:1]).then_inc(S["s4q"], 1)

            def s_evacs(gtb):
                p_ = gtb // ntbf
                tb = gtb % ntbf
                if p_ >= 2 and (p_ - 2, tb) in LASTTB:
                    sc.wait_ge(S["t4f"], LASTTB[(p_ - 2, tb)] + 1)
                for gp in range(NG):
                    gi = gtb * NG + gp
                    sc.wait_ge(S["t4t"], gi + 1)
                    sc.activation(
                        xn2Ts[p_ % 2][:, gp * 4:gp * 4 + 4, tb * 128:(tb + 1) * 128],
                        pb[6 + gi % 2][:].rearrange("p (a b) -> p a b", a=4),
                        AF.Copy).then_inc(S["s4e"], 1)

            def s_sig(gtb):
                g2 = gtb % 2
                sc.wait_ge(S["v4c"], gtb * CPT + 5)
                sc.activation(w12[:, 2 * g2:2 * g2 + 1],
                              m1s[:, 4 * g2 + 2:4 * g2 + 3], AF.Sigmoid
                              ).then_inc(S["s4g"], 1)

            sfidx = 0
            for kind, a1, a2, a3 in GS:
                if kind == "chain":
                    g = a1
                    s_stats(g)
                    if g >= 1:
                        s_sqrt(g - 1)
                    if g >= 2:
                        s_sig(g - 2)
                    if g >= 1:
                        s_evacs(g - 1)
                elif kind == "flush":
                    s_sqrt(TT - 1)
                    s_sig(TT - 2)
                    s_evacs(TT - 1)
                    s_sig(TT - 1)
                elif kind == "ffn":
                    p, och, tb = a1, a2, a3
                    gtb = p * ntbf + tb
                    sc.wait_ge(S["g4f"], sfidx + 1)
                    sc.dma_start(
                        y[gtb * 128:(gtb + 1) * 128, och * OCH:(och + 1) * OCH],
                        out_st[:, (sfidx % 2) * OCH:(sfidx % 2 + 1) * OCH]
                    ).then_inc(S[f"y4s{sfidx % 2}"], 16)
                    sfidx += 1
            for j in range(2):
                sc.wait_ge(S[f"y4s{j}"], 16 * (FI // 2 + (1 if FI % 2 > j else 0)))

    return nc


def host_layout(cfg, p, Wq, Wk, Wv, W_base, bq, bk, bv, b_base, bias_weight,
                W_router, A, B):
    D, CW, OW = cfg["D"], cfg["CW"], cfg["OW"]
    nk, ncbl = cfg["nk"], cfg["ncbl"]
    NVCHL, VCH, NOCH, OCH = cfg["NVCHL"], cfg["VCH"], cfg["NOCH"], cfg["OCH"]
    NHL = cfg["NHL"]
    chs = slice(p * CW, (p + 1) * CW)
    osl = slice(p * OW, (p + 1) * OW)

    def tile_wt(WT, nch, chw):
        return np.ascontiguousarray(
            WT.reshape(nk, 128, nch, chw).transpose(2, 1, 0, 3))

    out = {}
    out["wqt"] = tile_wt(np.ascontiguousarray(Wq.T[:, chs]), ncbl, 128)
    out["wkt"] = tile_wt(np.ascontiguousarray(Wk.T[:, chs]), ncbl, 128)
    out["wvt"] = tile_wt(np.ascontiguousarray(Wv.T[:, chs]), NVCHL, VCH)
    out["wbt"] = tile_wt(np.ascontiguousarray(W_base.T[:, osl]), NOCH, OCH)
    A_allT = np.ascontiguousarray(A.reshape(cfg["NE"] * cfg["R"], D).T)
    out["at"] = np.ascontiguousarray(
        A_allT.reshape(nk, 128, 128).transpose(1, 0, 2))
    out["bt"] = np.ascontiguousarray(
        B.transpose(0, 2, 1).reshape(cfg["NE"] * cfg["R"], D)[:, osl])
    WrT = np.ascontiguousarray(W_router.T)
    out["wrt"] = np.ascontiguousarray(
        WrT.reshape(nk, 128, cfg["NE"]).transpose(1, 0, 2))
    out["bqp"] = np.ascontiguousarray(bq[chs].reshape(ncbl, 128).T)
    out["bkp"] = np.ascontiguousarray(bk[chs].reshape(ncbl, 128).T)
    bvz = np.zeros((128, CW), dtype=np.float32)
    bvz[0] = bv[chs]
    out["bvr"] = bvz
    bbz = np.zeros((128, OW), dtype=np.float32)
    bbz[0] = b_base[osl]
    out["bbr"] = bbz
    out["bwt"] = np.ascontiguousarray(bias_weight[p * NHL:(p + 1) * NHL].T)
    out.update(host_consts(cfg))
    return out


def kernel(L0_attn_x, L0_attn_Wq, L0_attn_bq, L0_attn_Wk, L0_attn_bk,
           L0_attn_Wv, L0_attn_bv, L0_attn_bias_weight, L0_ffn_W_base,
           L0_ffn_b_base, L0_ffn_W_router, L0_ffn_A, L0_ffn_B):
    cfg = make_cfg(D=4096, L=2048, NH=16, CD=32)
    OW = cfg["OW"]
    x = np.asarray(L0_attn_x)
    Bz, L, D = x.shape

    args = (np.asarray(L0_attn_Wq), np.asarray(L0_attn_Wk),
            np.asarray(L0_attn_Wv), np.asarray(L0_ffn_W_base),
            np.asarray(L0_attn_bq), np.asarray(L0_attn_bk),
            np.asarray(L0_attn_bv), np.asarray(L0_ffn_b_base),
            np.asarray(L0_attn_bias_weight), np.asarray(L0_ffn_W_router),
            np.asarray(L0_ffn_A), np.asarray(L0_ffn_B))
    shared = [host_layout(cfg, p, *args) for p in range(2)]

    in_maps = []
    for c in range(8):
        b, p = c // 2, c % 2
        m = dict(shared[p])
        m["xc"] = np.ascontiguousarray(x[b])
        m["xres"] = np.ascontiguousarray(x[b][:, p * OW:(p + 1) * OW])
        in_maps.append(m)

    nc = build_program(cfg)
    res = run_bass_kernel_spmd(nc, in_maps, core_ids=list(range(8)))

    out = np.empty((Bz, L, D), dtype=np.float32)
    for c in range(8):
        b, p = c // 2, c % 2
        out[b, :, p * OW:(p + 1) * OW] = res.results[c]["y"]
    return out


# version 12
# speedup vs baseline: 1.0765x; 1.0765x over previous
import numpy as np
import ml_dtypes
from contextlib import ExitStack

import concourse.bass as bass
import concourse.mybir as mybir
from concourse.bass_utils import run_bass_kernel_spmd

F32 = mybir.dt.float32
BF16 = mybir.dt.bfloat16
U8 = mybir.dt.uint8
AF = mybir.ActivationFunctionType
ALU = mybir.AluOpType
AX = mybir.AxisListType

BF = ml_dtypes.bfloat16

SHIFT = 18.0
EPS = 1e-5


def _build_tables(g=(1.0, 1.0, 1.0)):
    dim = len(g)
    n = 2 ** dim
    masks = sorted(range(n), key=lambda m: (bin(m).count('1'), m))
    inv = {m: i for i, m in enumerate(masks)}
    C = np.zeros((n, n, n), dtype=np.float32)
    for i, a in enumerate(masks):
        for j, b in enumerate(masks):
            sw, at = (0, a >> 1)
            while at:
                sw += bin(at & b).count('1')
                at >>= 1
            s = -1.0 if sw & 1 else 1.0
            for t in range(dim):
                if a >> t & 1 and b >> t & 1:
                    s *= float(g[t])
            C[i, j, inv[a ^ b]] = s
    grades = np.array([bin(m).count('1') for m in masks])
    rev = np.where(grades * (grades - 1) // 2 % 2 == 1, -1.0, 1.0).astype(np.float32)
    met = np.array([float(np.prod([g[t] for t in range(dim) if m >> t & 1] + [1.0]))
                    for m in masks], dtype=np.float32)
    return C, rev, met


def make_cfg(D=4096, L=2048, NH=16, CD=32):
    NB = 8
    cfg = dict(D=D, L=L, NH=NH, CD=CD, NB=NB, NE=8, R=16, SCALING=2.0)
    cfg["HCH"] = CD * NB
    cfg["CBH"] = cfg["HCH"] // 128
    cfg["VW"] = cfg["HCH"] + 1
    cfg["NHL"] = NH // 2
    cfg["CW"] = cfg["NHL"] * cfg["HCH"]
    cfg["ncbl"] = cfg["CW"] // 128
    cfg["nk"] = D // 128
    cfg["ntb"] = L // 128
    cfg["TCQ"] = min(512, L)
    cfg["NQC"] = L // cfg["TCQ"]
    cfg["QBC"] = cfg["TCQ"] // 128
    cfg["TCK"] = min(512, L)
    cfg["NTCK"] = L // cfg["TCK"]
    cfg["VCH"] = min(256, cfg["CW"])
    cfg["NVCHL"] = cfg["CW"] // cfg["VCH"]
    cfg["OW"] = D // 2
    cfg["OCH"] = min(256, cfg["OW"])
    cfg["NOCH"] = cfg["OW"] // cfg["OCH"]
    cfg["LF"] = L // 2
    cfg["ntbf"] = cfg["LF"] // 128
    cfg["TCF"] = min(256, cfg["LF"])
    cfg["NQCF"] = cfg["LF"] // cfg["TCF"]
    cfg["NBS"] = max(1, D // 512)
    return cfg


def host_consts(cfg):
    C, rev, met = _build_tables()
    NHL, NB = cfg["NHL"], cfg["NB"]
    scale = 1.0 / np.sqrt(cfg["CD"] * NB)
    biv = C[:, :, [4, 5, 6]]
    bkt3 = np.ascontiguousarray(biv.transpose(2, 0, 1).reshape(3, NB * NB)).astype(np.float32)
    revd = np.tile(np.repeat(rev, NB)[None, :], (NHL, 1)).astype(np.float32)
    d1row = np.zeros(NB * NB, dtype=np.float32)
    for i in range(NB):
        d1row[i * NB + i] = rev[i] * scale * met[i]
    d1 = np.tile(d1row[None, :], (NHL, 1)).astype(np.float32)
    mask = np.zeros((128, 128), dtype=np.float32)
    for a in range(128):
        mask[a, (a // NB) * NB:(a // NB) * NB + NB] = 1.0
    rep = np.zeros((NB, 128), dtype=np.float32)
    for a in range(128):
        rep[a % NB, a] = 1.0
    exp8 = np.zeros((cfg["NE"], 128), dtype=np.float32)
    for er in range(cfg["NE"] * cfg["R"]):
        exp8[er // cfg["R"], er] = cfg["SCALING"]
    return dict(
        ibf=np.eye(128, dtype=np.float32).astype(BF),
        if32=np.eye(128, dtype=np.float32),
        rep=rep, mask=mask, exp8=exp8,
        row0=np.concatenate([np.ones((1, 128), dtype=np.float32),
                             np.zeros((127, 128), dtype=np.float32)]).astype(BF),
        bkt3=bkt3, revd=revd, d1=d1,
        ceps=np.full((128, 1), EPS, dtype=np.float32),
        cshift=np.full((128, 1), -SHIFT, dtype=np.float32),
    )


def build_program(cfg):
    D, L = cfg["D"], cfg["L"]
    NHL, HCH, CBH, VW = cfg["NHL"], cfg["HCH"], cfg["CBH"], cfg["VW"]
    NB, NE, CW = cfg["NB"], cfg["NE"], cfg["CW"]
    nk, ncbl, ntb = cfg["nk"], cfg["ncbl"], cfg["ntb"]
    TCQ, NQC, QBC = cfg["TCQ"], cfg["NQC"], cfg["QBC"]
    TCK, NTCK = cfg["TCK"], cfg["NTCK"]
    VCH, NVCHL = cfg["VCH"], cfg["NVCHL"]
    OW, OCH, NOCH = cfg["OW"], cfg["OCH"], cfg["NOCH"]
    LF, ntbf, TCF, NQCF = cfg["LF"], cfg["ntbf"], cfg["TCF"], cfg["NQCF"]
    NBS = cfg["NBS"]
    NG = nk // 4

    nc = bass.Bass("TRN2", target_bir_lowering=False)

    xc = nc.dram_tensor("xc", [L, D], F32, kind="ExternalInput").ap()
    xres = nc.dram_tensor("xres", [L, OW], F32, kind="ExternalInput").ap()
    wqt = nc.dram_tensor("wqt", [ncbl, 128, nk, 128], F32, kind="ExternalInput").ap()
    wkt = nc.dram_tensor("wkt", [ncbl, 128, nk, 128], F32, kind="ExternalInput").ap()
    wvt = nc.dram_tensor("wvt", [NVCHL, 128, nk, VCH], F32, kind="ExternalInput").ap()
    wbt = nc.dram_tensor("wbt", [NOCH, 128, nk, OCH], F32, kind="ExternalInput").ap()
    at_d = nc.dram_tensor("at", [128, nk, 128], F32, kind="ExternalInput").ap()
    bt_d = nc.dram_tensor("bt", [128, OW], F32, kind="ExternalInput").ap()
    wrt = nc.dram_tensor("wrt", [128, nk, NE], F32, kind="ExternalInput").ap()
    bqp = nc.dram_tensor("bqp", [128, ncbl], F32, kind="ExternalInput").ap()
    bkp = nc.dram_tensor("bkp", [128, ncbl], F32, kind="ExternalInput").ap()
    bvr = nc.dram_tensor("bvr", [128, CW], F32, kind="ExternalInput").ap()
    bbr = nc.dram_tensor("bbr", [128, OW], F32, kind="ExternalInput").ap()
    bwt_d = nc.dram_tensor("bwt", [3, NHL], F32, kind="ExternalInput").ap()
    ibf_d = nc.dram_tensor("ibf", [128, 128], BF16, kind="ExternalInput").ap()
    if32_d = nc.dram_tensor("if32", [128, 128], F32, kind="ExternalInput").ap()
    rep_d = nc.dram_tensor("rep", [NB, 128], F32, kind="ExternalInput").ap()
    mask_d = nc.dram_tensor("mask", [128, 128], F32, kind="ExternalInput").ap()
    exp8_d = nc.dram_tensor("exp8", [NE, 128], F32, kind="ExternalInput").ap()
    row0_d = nc.dram_tensor("row0", [128, 128], BF16, kind="ExternalInput").ap()
    bkt3_d = nc.dram_tensor("bkt3", [3, NB * NB], F32, kind="ExternalInput").ap()
    revd_d = nc.dram_tensor("revd", [NHL, NB * NB], F32, kind="ExternalInput").ap()
    d1_d = nc.dram_tensor("d1", [NHL, NB * NB], F32, kind="ExternalInput").ap()
    ceps_d = nc.dram_tensor("ceps", [128, 1], F32, kind="ExternalInput").ap()
    cshift_d = nc.dram_tensor("cshift", [128, 1], F32, kind="ExternalInput").ap()

    y = nc.dram_tensor("y", [L, OW], F32, kind="ExternalOutput").ap()
    qts = nc.dram_tensor("qts", [ncbl, 128, L], BF16, kind="Internal").ap()
    kts = nc.dram_tensor("kts", [ncbl, 128, L], BF16, kind="Internal").ap()
    vs = nc.dram_tensor("vs", [ntb, 128, CW], BF16, kind="Internal").ap()
    xap = nc.dram_tensor("xap", [ntb, 128, CW], BF16, kind="Internal").ap()
    xag1 = nc.dram_tensor("xag1", [2 * ntbf * 128, CW], BF16, kind="Internal").ap()
    xag2 = nc.dram_tensor("xag2", [2 * ntbf * 128, CW], BF16, kind="Internal").ap()
    talld = nc.dram_tensor("talld", [NHL, NB * NB], F32, kind="Internal").ap()

    es = ExitStack()
    with es:
        def sb(name, shape, dt):
            return es.enter_context(nc.sbuf_tensor(name, shape, dt))

        ibf = sb("ibf_s", [128, 128], BF16)
        if32 = sb("if32_s", [128, 128], F32)
        rep_s = sb("rep_s", [NB, 128], F32)
        mask_s = sb("mask_s", [128, 128], F32)
        exp8_s = sb("exp8_s", [NE, 128], F32)
        row0_s = sb("row0_s", [128, 128], BF16)
        bkt3 = sb("bkt3_s", [3, NB * NB], F32)
        revd = sb("revd_s", [NHL, NB * NB], F32)
        d1s = sb("d1_s", [NHL, NB * NB], F32)
        bwt_s = sb("bwt_s", [3, NHL], F32)
        ceps_s = sb("ceps_s", [128, 1], F32)
        cshift_s = sb("cshift_s", [128, 1], F32)
        G_sb = sb("G_sb", [128, NHL * 128], BF16)
        bkp_s = sb("bkp_s", [128, ncbl], F32)
        bqbf = sb("bqbf", [128, ncbl], BF16)
        bqeff = sb("bqeff", [128, ncbl], F32)
        ttile = sb("ttile", [NB, NHL * NB], F32)
        tmp64 = sb("tmp64", [NHL, NB * NB], F32)
        tall = sb("tall", [NHL, NB * NB], F32)
        trow = sb("trow", [128, 2 * NB], F32)
        stats = sb("stats", [128, 6 * NBS], F32)
        mv = sb("mv", [128, 8], F32)
        stdv = sb("stdv", [128, 4], F32)
        rstd = sb("rstd", [128, 4], F32)
        recip = sb("recip", [128, QBC], F32)
        comb = sb("comb", [128, 2 * NE], F32)
        m1s = sb("m1s", [128, 8], F32)
        st4 = sb("st4", [128, 8], F32)
        w12 = sb("w12", [128, 4], F32)
        r2s = sb("r2s", [128, 2 * NE], F32)
        is1s = sb("is1s", [128, 2 * NE], F32)
        is2s = sb("is2s", [128, 2 * NE], F32)

        AB = 192 * 1024
        arena = sb("arena", [128, AB], U8)

        def carve(off, nbytes, dt):
            off = (off + 31) // 32 * 32
            assert off + nbytes <= AB, (off, nbytes)
            return arena[:, off:off + nbytes].bitcast(dt), off + nbytes

        xn1T_f, o = carve(0, nk * L * 2, BF16)
        xn1T = xn1T_f.rearrange("p (k t) -> p k t", k=nk)
        PH = (o + 31) // 32 * 32
        o = PH
        x_tile0, o = carve(o, D * 4, F32)
        x_tile1, o = carve(o, D * 4, F32)
        xn_tile0, o = carve(o, D * 2, BF16)
        xn_tile1, o = carve(o, D * 2, BF16)
        assert o <= AB
        x_tiles = (x_tile0, x_tile1)
        xn_tiles = (xn_tile0, xn_tile1)
        o = PH
        wq0, o = carve(o, nk * 128 * 2, BF16)
        wq1, o = carve(o, nk * 128 * 2, BF16)
        wv0, o = carve(o, nk * VCH * 2, BF16)
        wv1, o = carve(o, nk * VCH * 2, BF16)
        qraw, o = carve(o, 2 * TCQ * 2, BF16)
        stage, o = carve(o, 4 * 512 * 2, BF16)
        bvrbf, o = carve(o, CW * 2, BF16)
        assert o <= AB
        o = 0
        q_sb, o = carve(o, 2 * CBH * L * 2, BF16)
        k_sb, o = carve(o, 2 * CBH * L * 2, BF16)
        v_sb, o = carve(o, 2 * ntb * VW * 2, BF16)
        attnT, o = carve(o, 2 * ntb * TCQ * 2, BF16)
        xstg, o = carve(o, 4 * HCH * 2, BF16)
        PH3_TOP = o
        assert o <= AB
        o = 0
        xn2T_f, o = carve(o, nk * LF * 2, BF16)
        xn2T = xn2T_f.rearrange("p (k t) -> p k t", k=nk)
        wb0, o = carve(o, nk * OCH * 2, BF16)
        wb1, o = carve(o, nk * OCH * 2, BF16)
        a_sb, o = carve(o, nk * 128 * 2, BF16)
        b_sb, o = carve(o, OW * 2, BF16)
        wr_sb, o = carve(o, nk * NE * 2, BF16)
        X4_OFF = (o + 31) // 32 * 32
        x4_tile, o = carve(o, D * 4, F32)
        xa4_tile, o = carve(o, D * 2, BF16)
        xmid, o = carve(o, D * 4, F32)
        xn2_tile0, o = carve(o, D * 2, BF16)
        xn2_tile1, o = carve(o, D * 2, BF16)
        xn2_tiles = (xn2_tile0, xn2_tile1)
        hwT, o = carve(o, LF * 2, BF16)
        combexp, o = carve(o, LF * 4, F32)
        combT, o = carve(o, LF * 4, F32)
        bbrbf, o = carve(o, OW * 2, BF16)
        xa_sl, o = carve(o, 4 * OCH * 2, BF16)
        x_sl, o = carve(o, 4 * OCH * 4, F32)
        t1_st, o = carve(o, 2 * OCH * 4, F32)
        out_st, o = carve(o, 2 * OCH * 4, F32)
        assert o <= AB

        pb = [es.enter_context(nc.psum_tensor(f"pb{i}", [128, 512], F32))
              for i in range(8)]

        names = ("bar c_ld c_gl d0 t0 v0 m3 "
                 "x1le x1lo v1a s1q v1r v1n t1 v1e "
                 "w2le w2lo t2 v2a t2g e2 w2s0 w2s1 w2s2 w2s3 "
                 "q3le q3lo t3l s3e t3o v3r s3n w3s0 w3s1 w3s2 w3s3 ccs "
                 "x4l v4m s4a v4a s4q v4r v4n t4t v4e t4r v4c s4g t4c v4ct "
                 "t4x v4x t4h v4h g4a g4le g4lo t4f v4f s1e s4e "
                 "x4c0 x4c1 x4c2 x4c3 a4c0 a4c1 a4c2 a4c3 g4f y4s0 y4s1").split()
        S = {n: es.enter_context(nc.semaphore(n)) for n in names}
        block = es.enter_context(nc.Block())

        NC_LD = 14
        if NQC >= 2:
            _halves = [list(range(0, NQC // 2)), list(range(NQC // 2, NQC))]
        else:
            _halves = [list(range(NQC))]
        ATT_VH = []
        vhi = 0
        for hf in range(len(_halves)):
            for h in range(NHL):
                ATT_VH.append((vhi, h, _halves[hf]))
                vhi += 1
        NQCV = len(_halves[0])
        OGV = NQCV * QBC
        QI = ncbl * NQC
        KI = ncbl * NTCK
        VI = NVCHL * ntb
        E2T = QI + KI + VI
        LD3 = 2 * CBH + ntb
        OG = NQC * QBC
        TOG = NHL * OG
        CPT = 9
        FIP = NOCH * ntbf
        FI = 2 * FIP
        CHTB = TCF // 128
        SCHED = []
        pend = []
        for tb in range(ntbf):
            k = 3
            while pend and k > 0:
                och, tb2 = pend.pop(0)
                SCHED.append(("ffn", och, tb2))
                k -= 1
            SCHED.append(("chain", tb, 0))
            if (tb + 1) % CHTB == 0:
                c = tb // CHTB
                SCHED.append(("chunk", c, 0))
                for och in range(min(2, NOCH)):
                    for tb2 in range(c * CHTB, (c + 1) * CHTB):
                        pend.append((och, tb2))
        while pend:
            och, tb2 = pend.pop(0)
            SCHED.append(("ffn", och, tb2))
        for och in range(min(2, NOCH), NOCH):
            for tb in range(ntbf):
                SCHED.append(("ffn", och, tb))
        FFN_ORDER = [(och, tb) for kind, och, tb in SCHED if kind == "ffn"]
        assert len(FFN_ORDER) == FIP
        LASTIDX = {}
        for i, (och, tb) in enumerate(FFN_ORDER):
            LASTIDX[och] = i

        @block.sync
        def _(s):
            for dst, src in ((ibf, ibf_d), (if32, if32_d), (rep_s, rep_d),
                             (mask_s, mask_d), (exp8_s, exp8_d),
                             (row0_s, row0_d), (bkt3, bkt3_d), (revd, revd_d),
                             (d1s, d1_d), (bwt_s, bwt_d), (bkp_s, bkp),
                             (ceps_s, ceps_d), (cshift_s, cshift_d)):
                s.dma_start(dst[:], src).then_inc(S["c_ld"], 16)
            s.dma_start(m1s[:, 3:4], ceps_d).then_inc(S["c_ld"], 16)
            s.wait_ge(S["v0"], 3)
            s.dma_start(talld, tall[:]).then_inc(S["d0"], 16)
            s.wait_ge(S["d0"], 16)
            s.dma_start(ttile[:].rearrange("i (h j) -> i h j", h=NHL),
                        talld.rearrange("h (i j) -> i h j", i=NB)
                        ).then_inc(S["d0"], 16)
            s.wait_ge(S["d0"], 32)
            s.sem_inc(S["bar"], 1)
            s.wait_ge(S["bar"], 5)
            for tb in range(ntb):
                if tb >= 2:
                    s.wait_ge(S["v1n"], tb - 1)
                s.dma_start(x_tiles[tb % 2], xc[tb * 128:(tb + 1) * 128, :]
                            ).then_inc(S["x1le" if tb % 2 == 0 else "x1lo"], 16)
            s.wait_ge(S["s1e"], ntb * NG)
            s.sem_inc(S["bar"], 1)
            s.wait_ge(S["bar"], 10)
            sidx = 0
            for cb in range(ncbl):
                for tc in range(NQC):
                    s.wait_ge(S["e2"], sidx + 1)
                    s.dma_start(qts[cb, :, tc * TCQ:(tc + 1) * TCQ],
                                stage[:, (sidx % 4) * 512:(sidx % 4) * 512 + TCQ]
                                ).then_inc(S[f"w2s{sidx % 4}"], 16)
                    sidx += 1
            for cb in range(ncbl):
                for tc in range(NTCK):
                    s.wait_ge(S["e2"], sidx + 1)
                    s.dma_start(kts[cb, :, tc * TCK:(tc + 1) * TCK],
                                stage[:, (sidx % 4) * 512:(sidx % 4) * 512 + TCK]
                                ).then_inc(S[f"w2s{sidx % 4}"], 16)
                    sidx += 1
            for ch in range(NVCHL):
                for tb in range(ntb):
                    s.wait_ge(S["e2"], sidx + 1)
                    s.dma_start(vs[tb, :, ch * VCH:(ch + 1) * VCH],
                                stage[:, (sidx % 4) * 512:(sidx % 4) * 512 + VCH]
                                ).then_inc(S[f"w2s{sidx % 4}"], 16)
                    sidx += 1
            for j in range(4):
                s.wait_ge(S[f"w2s{j}"], 16 * (E2T // 4 + (1 if E2T % 4 > j else 0)))
            s.sem_inc(S["bar"], 1)
            s.wait_ge(S["bar"], 15)
            s.wait_ge(S["t2"], E2T)
            for vh, h, qcs in ATT_VH:
                hb = vh % 2
                if vh >= 2:
                    s.wait_ge(S["t3o"], OGV * (vh - 1))
                q3 = "q3le" if vh % 2 == 0 else "q3lo"
                for cc in range(CBH):
                    s.dma_start(
                        q_sb[:, hb * CBH * L + cc * L:hb * CBH * L + (cc + 1) * L],
                        qts[h * CBH + cc]).then_inc(S[q3], 16)
                    s.dma_start(
                        k_sb[:, hb * CBH * L + cc * L:hb * CBH * L + (cc + 1) * L],
                        kts[h * CBH + cc]).then_inc(S[q3], 16)
                for mb in range(ntb):
                    s.dma_start(
                        v_sb[:, hb * ntb * VW + mb * VW:
                             hb * ntb * VW + mb * VW + HCH],
                        vs[mb, :, h * HCH:(h + 1) * HCH],
                    ).then_inc(S[q3], 16)
            s.sem_inc(S["bar"], 1)
            if X4_OFF < PH3_TOP:
                s.wait_ge(S["t3o"], TOG)
                for j in range(4):
                    s.wait_ge(S[f"w3s{j}"],
                              16 * (TOG // 4 + (1 if TOG % 4 > j else 0)))
            for ps in range(2):
                s.wait_ge(S["ccs"], ps + 1)
                xagN = xag1 if ps == 0 else xag2
                base = ps * FIP
                fpos = 0
                for kind, a1, a2 in SCHED:
                    if kind == "chain":
                        tb = a1
                        gtb = ps * ntbf + tb
                        if gtb >= 1:
                            s.wait_ge(S["v4m"], gtb)
                            s.wait_ge(S["s4a"], 2 * gtb)
                        s.dma_start(x4_tile, xc[gtb * 128:(gtb + 1) * 128, :]
                                    ).then_inc(S["x4l"], 16)
                        s.dma_start(xa4_tile[:, 0:CW],
                                    xagN[tb * 128:(tb + 1) * 128, :]
                                    ).then_inc(S["x4l"], 16)
                        s.dma_start(xa4_tile[:, CW:D],
                                    xagN[(ntbf + tb) * 128:(ntbf + tb + 1) * 128, :]
                                    ).then_inc(S["x4l"], 16)
                    elif kind == "ffn":
                        och, tb = a1, a2
                        fidx = base + fpos
                        gtb = ps * ntbf + tb
                        if fidx >= 4:
                            s.wait_ge(S["v4f"], fidx - 3)
                            s.wait_ge(S["g4f"], fidx - 3)
                        s.dma_start(
                            xa_sl[:, (fidx % 4) * OCH:(fidx % 4 + 1) * OCH],
                            xap[gtb, :, och * OCH:(och + 1) * OCH]
                        ).then_inc(S[f"a4c{fidx % 4}"], 16)
                        s.dma_start(
                            x_sl[:, (fidx % 4) * OCH:(fidx % 4 + 1) * OCH],
                            xres[gtb * 128:(gtb + 1) * 128, och * OCH:(och + 1) * OCH]
                        ).then_inc(S[f"x4c{fidx % 4}"], 16)
                        fpos += 1

        @block.gpsimd
        def _(g):
            g.dma_start(bqbf[:], bqp).then_inc(S["c_gl"], 16)
            g.dma_start(bvrbf[:], bvr).then_inc(S["c_gl"], 16)
            g.sem_inc(S["bar"], 1)
            g.wait_ge(S["bar"], 5)
            g.sem_inc(S["bar"], 1)
            g.wait_ge(S["bar"], 10)
            g.wait_ge(S["v1n"], ntb)
            g.wait_ge(S["t1"], ntb * NG)
            for cb in range(ncbl):
                if cb >= 2:
                    g.wait_ge(S["t2"], NQC * (cb - 1))
                g.dma_start((wq0 if cb % 2 == 0 else wq1)[:],
                            wqt[cb].rearrange("p k c -> p (k c)")
                            ).then_inc(S["w2le" if cb % 2 == 0 else "w2lo"], 16)
            for cb in range(ncbl):
                if cb == 0:
                    g.wait_ge(S["t2"], (ncbl - 1) * NQC)
                elif cb == 1:
                    g.wait_ge(S["t2"], QI)
                else:
                    g.wait_ge(S["t2"], QI + NTCK * (cb - 1))
                g.dma_start((wq0 if cb % 2 == 0 else wq1)[:],
                            wkt[cb].rearrange("p k c -> p (k c)")
                            ).then_inc(S["w2le" if cb % 2 == 0 else "w2lo"], 16)
            for ch in range(NVCHL):
                if ch == 0:
                    g.wait_ge(S["w2le"], 16 * ncbl)
                elif ch == 1:
                    g.wait_ge(S["w2lo"], 16 * ncbl)
                else:
                    g.wait_ge(S["t2"], QI + KI + ntb * (ch - 1))
                g.dma_start((wv0 if ch % 2 == 0 else wv1)[:],
                            wvt[ch].rearrange("p k c -> p (k c)")
                            ).then_inc(S["w2le" if ch % 2 == 0 else "w2lo"], 16)
            g.sem_inc(S["bar"], 1)
            g.wait_ge(S["bar"], 15)
            g.sem_inc(S["bar"], 1)
            slot_seq = [[] for _ in range(4)]
            _qcg = 0
            for vh, h, qcs in ATT_VH:
                for qc in qcs:
                    for qb in range(QBC):
                        og = _qcg * QBC + qb
                        slot_seq[og % 4].append(qc * QBC + qb)
                    _qcg += 1
            for j in range(4):
                fh = [i for i, tb_ in enumerate(slot_seq[j]) if tb_ < ntbf]
                thr = (fh[-1] + 1) if fh else 0
                if thr:
                    g.wait_ge(S[f"w3s{j}"], 16 * thr)
            g.collective_compute(
                "AllGather", mybir.AluOpType.bypass,
                replica_groups=[[0, 1], [2, 3], [4, 5], [6, 7]],
                ins=[xap[0:ntbf].rearrange("m p c -> (m p) c").opt()],
                outs=[xag1.opt()],
            ).then_inc(S["ccs"], 1)
            g.wait_ge(S["t3o"], TOG)
            for j in range(4):
                g.wait_ge(S[f"w3s{j}"], 16 * len(slot_seq[j]))
            g.collective_compute(
                "AllGather", mybir.AluOpType.bypass,
                replica_groups=[[0, 1], [2, 3], [4, 5], [6, 7]],
                ins=[xap[ntbf:ntb].rearrange("m p c -> (m p) c").opt()],
                outs=[xag2.opt()],
            ).then_inc(S["ccs"], 1)
            g.dma_start(a_sb[:], at_d.rearrange("p k c -> p (k c)")).then_inc(S["g4a"], 16)
            g.dma_start(b_sb[:], bt_d).then_inc(S["g4a"], 16)
            g.dma_start(wr_sb[:], wrt.rearrange("p k c -> p (k c)")).then_inc(S["g4a"], 16)
            g.dma_start(bbrbf[:], bbr).then_inc(S["g4a"], 16)
            for ps in range(2):
                base = ps * FIP
                if ps == 0:
                    g.dma_start(wb0[:], wbt[0].rearrange("p k c -> p (k c)")
                                ).then_inc(S["g4le"], 16)
                    if NOCH > 1:
                        g.dma_start(wb1[:], wbt[1].rearrange("p k c -> p (k c)")
                                    ).then_inc(S["g4lo"], 16)
                else:
                    g.wait_ge(S["t4f"], LASTIDX[0] + 1)
                    g.dma_start(wb0[:], wbt[0].rearrange("p k c -> p (k c)")
                                ).then_inc(S["g4le"], 16)
                    if NOCH > 1:
                        g.wait_ge(S["t4f"], LASTIDX[1] + 1)
                        g.dma_start(wb1[:], wbt[1].rearrange("p k c -> p (k c)")
                                    ).then_inc(S["g4lo"], 16)
                for fpos, (och, tb) in enumerate(FFN_ORDER):
                    idx = base + fpos
                    g.wait_ge(S["v4f"], idx + 1)
                    g.wait_ge(S[f"x4c{idx % 4}"], 16 * (idx // 4 + 1))
                    if idx >= 2:
                        g.wait_ge(S[f"y4s{idx % 2}"], 16 * (idx // 2))
                    g.tensor_add(
                        out_st[:, (idx % 2) * OCH:(idx % 2 + 1) * OCH],
                        t1_st[:, (idx % 2) * OCH:(idx % 2 + 1) * OCH],
                        x_sl[:, (idx % 4) * OCH:(idx % 4 + 1) * OCH],
                    ).then_inc(S["g4f"], 1)
                    for pf in range(NOCH):
                        if pf + 2 < NOCH and LASTIDX[pf] == fpos:
                            g.wait_ge(S["t4f"], base + LASTIDX[pf] + 1)
                            g.dma_start(
                                (wb0 if (pf + 2) % 2 == 0 else wb1)[:],
                                wbt[pf + 2].rearrange("p k c -> p (k c)")
                                ).then_inc(S["g4le" if (pf + 2) % 2 == 0 else "g4lo"], 16)

        @block.tensor
        def _(t):
            t.wait_ge(S["c_ld"], 16 * NC_LD)
            t.wait_ge(S["c_gl"], 32)
            t.matmul(pb[6][0:NHL, 0:NB * NB], bwt_s[:, 0:NHL], bkt3[:, :],
                     start=True, stop=True).then_inc(S["t0"], 1)
            t.wait_ge(S["d0"], 32)
            for h in range(NHL):
                t.wait_ge(S["v0"], 1 if h == 0 else 3 + 2 * (h - 1) + 1)
                t.matmul(pb[6][:, 256 + h * NB:256 + (h + 1) * NB], rep_s[:, :],
                         ttile[:, h * NB:(h + 1) * NB],
                         start=True, stop=True).then_inc(S["t0"], 1)
            t.wait_ge(S["v0"], 3 + 2 * NHL)
            for cb in range(ncbl):
                h = cb // CBH
                mm = t.matmul(pb[7][:, cb:cb + 1],
                              G_sb[:, h * 128:(h + 1) * 128],
                              bqbf[:, cb:cb + 1], start=True, stop=True)
            mm.then_inc(S["t0"], 1)
            t.sem_inc(S["bar"], 1)
            t.wait_ge(S["bar"], 5)
            gidx = 0
            for tb in range(ntb):
                t.wait_ge(S["v1n"], tb + 1)
                for gp in range(NG):
                    if gidx >= 2:
                        t.wait_ge(S["s1e"], gidx - 1)
                    for j in range(4):
                        k = gp * 4 + j
                        mm = t.matmul(pb[gidx % 2][:, j * 128:(j + 1) * 128],
                                      xn_tiles[tb % 2][:, k * 128:(k + 1) * 128],
                                      ibf[:], start=True, stop=True)
                    mm.then_inc(S["t1"], 1)
                    gidx += 1
            t.sem_inc(S["bar"], 1)
            t.wait_ge(S["bar"], 10)
            idx = 0
            for cb in range(ncbl):
                wbuf = (wq0 if cb % 2 == 0 else wq1)
                h = cb // CBH
                t.wait_ge(S["w2le" if cb % 2 == 0 else "w2lo"], 16 * (cb // 2 + 1))
                for tc in range(NQC):
                    if idx >= 2:
                        t.wait_ge(S["v2a"], idx - 1)
                    for k in range(nk):
                        mm = t.matmul(pb[idx % 2][:, 0:TCQ],
                                      wbuf[:, k * 128:(k + 1) * 128],
                                      xn1T[:, k, tc * TCQ:(tc + 1) * TCQ],
                                      start=(k == 0), stop=(k == nk - 1))
                    mm.then_inc(S["t2"], 1)
                    t.wait_ge(S["v2a"], idx + 1)
                    if idx >= 2:
                        t.wait_ge(S["e2"], idx - 1)
                    t.matmul(pb[2 + idx % 2][:, 0:TCQ],
                             G_sb[:, h * 128:(h + 1) * 128],
                             qraw[:, (idx % 2) * TCQ:(idx % 2 + 1) * TCQ],
                             start=True, stop=True).then_inc(S["t2g"], 1)
                    idx += 1
            for cb in range(ncbl):
                wbuf = (wq0 if cb % 2 == 0 else wq1)
                t.wait_ge(S["w2le" if cb % 2 == 0 else "w2lo"],
                          16 * (ncbl // 2 + cb // 2 + 1))
                for tc in range(NTCK):
                    if idx >= 2:
                        t.wait_ge(S["e2"], idx - 1)
                    for k in range(nk):
                        mm = t.matmul(pb[idx % 2][:, 0:TCK],
                                      wbuf[:, k * 128:(k + 1) * 128],
                                      xn1T[:, k, tc * TCK:(tc + 1) * TCK],
                                      start=(k == 0), stop=(k == nk - 1))
                    mm.then_inc(S["t2"], 1)
                    idx += 1
            for ch in range(NVCHL):
                vbuf = (wv0 if ch % 2 == 0 else wv1)
                t.wait_ge(S["w2le" if ch % 2 == 0 else "w2lo"],
                          16 * (ncbl + ch // 2 + 1))
                for tb in range(ntb):
                    if idx >= 2:
                        t.wait_ge(S["e2"], idx - 1)
                    for k in range(nk):
                        t.matmul(pb[idx % 2][:, 0:VCH],
                                 xn1T[:, k, tb * 128:(tb + 1) * 128],
                                 vbuf[:, k * VCH:(k + 1) * VCH],
                                 start=(k == 0), stop=False)
                    t.matmul(pb[idx % 2][:, 0:VCH], row0_s[:],
                             bvrbf[:, ch * VCH:(ch + 1) * VCH],
                             start=False, stop=True).then_inc(S["t2"], 1)
                    idx += 1
            t.sem_inc(S["bar"], 1)
            t.wait_ge(S["bar"], 15)
            t.wait_ge(S["m3"], 2)
            qcg = -1
            prev = None

            def t_outs(oqcg, ohb, oqc):
                t.wait_ge(S["s3e"], ntb * (oqcg + 1))
                for qb in range(QBC):
                    if oqcg >= 1:
                        t.wait_ge(S["s3n"], (oqcg - 1) * QBC + qb + 1)
                    for mb in range(ntb):
                        mm = t.matmul(
                            pb[2 + qb][:, 0:VW],
                            attnT[:, (oqcg % 2) * ntb * TCQ + mb * TCQ + qb * 128:
                                  (oqcg % 2) * ntb * TCQ + mb * TCQ + qb * 128 + 128],
                            v_sb[:, ohb * ntb * VW + mb * VW:
                                 ohb * ntb * VW + (mb + 1) * VW],
                            start=(mb == 0), stop=(mb == ntb - 1))
                    mm.then_inc(S["t3o"], 1)

            for vh, h, qcs in ATT_VH:
                hb = vh % 2
                t.wait_ge(S["q3le" if vh % 2 == 0 else "q3lo"],
                          16 * LD3 * (vh // 2 + 1))
                for qc in qcs:
                    qcg += 1
                    for mb in range(ntb):
                        li = qcg * ntb + mb
                        if li >= 4:
                            t.wait_ge(S["s3e"], li - 3)
                        for cc in range(CBH):
                            mm = t.matmul(
                                pb[(0, 1, 6, 7)[li % 4]][:, 0:TCQ],
                                k_sb[:, hb * CBH * L + cc * L + mb * 128:
                                     hb * CBH * L + cc * L + mb * 128 + 128],
                                q_sb[:, hb * CBH * L + cc * L + qc * TCQ:
                                     hb * CBH * L + cc * L + qc * TCQ + TCQ],
                                start=(cc == 0), stop=(cc == CBH - 1))
                        mm.then_inc(S["t3l"], 1)
                    if prev is not None:
                        t_outs(*prev)
                    prev = (qcg, hb, qc)
            t_outs(*prev)
            t.sem_inc(S["bar"], 1)
            t.wait_ge(S["bar"], 20)
            gidx = 0
            for ps in range(2):
                base = ps * FIP
                fpos = 0
                for kind, a1, a2 in SCHED:
                    if kind == "chain":
                        tb = a1
                        gtb = ps * ntbf + tb
                        t.wait_ge(S["v4n"], gtb + 1)
                        for gp in range(NG):
                            if gidx >= 2:
                                t.wait_ge(S["s4e"], gidx - 1)
                            for j in range(4):
                                k = gp * 4 + j
                                mm = t.matmul(pb[6 + gidx % 2][:, j * 128:(j + 1) * 128],
                                              xn2_tiles[gtb % 2][:, k * 128:(k + 1) * 128],
                                              ibf[:], start=True, stop=True)
                            mm.then_inc(S["t4t"], 1)
                            gidx += 1
                        t.wait_ge(S["s4e"], (gtb + 1) * NG)
                        t.wait_ge(S["g4a"], 16 * 4)
                        if gtb >= 1:
                            t.wait_ge(S["v4c"], (gtb - 1) * CPT + 3)
                        for k in range(nk):
                            mm = t.matmul(pb[2][:, 0:NE],
                                          xn2T[:, k, tb * 128:(tb + 1) * 128],
                                          wr_sb[:, k * NE:(k + 1) * NE],
                                          start=(k == 0), stop=(k == nk - 1))
                        mm.then_inc(S["t4r"], 1)
                        t.wait_ge(S["v4c"], gtb * CPT + CPT)
                        if gtb >= 1:
                            t.wait_ge(S["v4ct"], gtb)
                        t.matmul(pb[3][0:NE, 0:128],
                                 comb[:, (gtb % 2) * NE:(gtb % 2 + 1) * NE],
                                 if32[:], start=True, stop=True).then_inc(S["t4c"], 1)
                    elif kind == "chunk":
                        c = a1
                        ce = ps * NQCF + c
                        t.wait_ge(S["v4ct"], ps * ntbf + (c + 1) * CHTB)
                        if ce >= 1:
                            t.wait_ge(S["v4x"], ce)
                        t.matmul(pb[4][:, 0:TCF], exp8_s[:, :],
                                 combT[0:NE, c * TCF:(c + 1) * TCF],
                                 start=True, stop=True).then_inc(S["t4x"], 1)
                        if ce >= 1:
                            t.wait_ge(S["v4h"], ce)
                        for k in range(nk):
                            mm = t.matmul(pb[5][:, 0:TCF],
                                          a_sb[:, k * 128:(k + 1) * 128],
                                          xn2T[:, k, c * TCF:(c + 1) * TCF],
                                          start=(k == 0), stop=(k == nk - 1))
                        mm.then_inc(S["t4h"], 1)
                    else:
                        och, tb = a1, a2
                        fidx = base + fpos
                        if och % 2 == 0:
                            t.wait_ge(S["g4le"], 16 * (ps * ((NOCH + 1) // 2) + och // 2 + 1))
                        else:
                            t.wait_ge(S["g4lo"], 16 * (ps * (NOCH // 2) + och // 2 + 1))
                        t.wait_ge(S["s4e"], (ps * ntbf + tb + 1) * NG)
                        t.wait_ge(S["v4h"], ps * NQCF + tb // CHTB + 1)
                        if fidx >= 2:
                            t.wait_ge(S["v4f"], fidx - 1)
                        wbuf = (wb0 if och % 2 == 0 else wb1)
                        for k in range(nk):
                            t.matmul(pb[fidx % 2][:, 0:OCH],
                                     xn2T[:, k, tb * 128:(tb + 1) * 128],
                                     wbuf[:, k * OCH:(k + 1) * OCH],
                                     start=(k == 0), stop=False)
                        t.matmul(pb[fidx % 2][:, 0:OCH],
                                 hwT[:, tb * 128:(tb + 1) * 128],
                                 b_sb[:, och * OCH:(och + 1) * OCH],
                                 start=False, stop=False)
                        t.matmul(pb[fidx % 2][:, 0:OCH], row0_s[:],
                                 bbrbf[:, och * OCH:(och + 1) * OCH],
                                 start=False, stop=True).then_inc(S["t4f"], 1)
                        fpos += 1

        @block.vector
        def _(v):
            v.wait_ge(S["t0"], 1)
            v.tensor_tensor(tmp64[:], pb[6][0:NHL, 0:NB * NB], revd[:],
                            op=ALU.mult).then_inc(S["v0"], 1)
            v.wait_ge(S["v0"], 1)
            v.tensor_tensor(tall[:], tmp64[:], d1s[:], op=ALU.add
                            ).then_inc(S["v0"], 1)
            v.sem_inc(S["v0"], 1)
            for h in range(NHL):
                v.wait_ge(S["t0"], 2 + h)
                v.tensor_copy(trow[:, (h % 2) * NB:(h % 2 + 1) * NB],
                              pb[6][:, 256 + h * NB:256 + (h + 1) * NB]
                              ).then_inc(S["v0"], 1)
                v.wait_ge(S["v0"], 3 + 2 * h + 1)
                for rp in range(128 // NB):
                    mm = v.tensor_tensor(
                        G_sb[:, h * 128 + rp * NB:h * 128 + (rp + 1) * NB],
                        trow[:, (h % 2) * NB:(h % 2 + 1) * NB],
                        mask_s[:, rp * NB:(rp + 1) * NB], op=ALU.mult)
                mm.then_inc(S["v0"], 1)
            v.wait_ge(S["t0"], 2 + NHL)
            v.tensor_copy(bqeff[:, 0:ncbl], pb[7][:, 0:ncbl])
            v.sem_inc(S["bar"], 1)
            v.wait_ge(S["bar"], 5)
            for tb in range(ntb):
                v.wait_ge(S["x1le" if tb % 2 == 0 else "x1lo"], 16 * (tb // 2 + 1))
                for j in range(NBS):
                    bs = v.bn_stats(stats[:, j * 6:(j + 1) * 6],
                                    x_tiles[tb % 2][:, j * 512:(j + 1) * 512])
                bs.then_inc(S["v1a"], 1)
                v.wait_ge(S["v1a"], 2 * tb + 1)
                v.bn_aggr(mv[:, 0:2], stats[:, 0:6 * NBS]).then_inc(S["v1a"], 1)
                v.wait_ge(S["s1q"], tb + 1)
                v.reciprocal(rstd[:, 0:1], stdv[:, 0:1]).then_inc(S["v1r"], 1)
                v.wait_ge(S["v1r"], tb + 1)
                if tb >= 2:
                    v.wait_ge(S["t1"], (tb - 1) * NG)
                v.tensor_scalar(xn_tiles[tb % 2][:], x_tiles[tb % 2][:],
                                mv[:, 0:1], rstd[:, 0:1],
                                op0=ALU.subtract, op1=ALU.mult
                                ).then_inc(S["v1n"], 1)
            v.sem_inc(S["bar"], 1)
            v.wait_ge(S["bar"], 10)
            idx = 0
            sidx = 0
            for cb in range(ncbl):
                for tc in range(NQC):
                    v.wait_ge(S["t2"], idx + 1)
                    if idx >= 2:
                        v.wait_ge(S["t2g"], idx - 1)
                    v.tensor_copy(qraw[:, (idx % 2) * TCQ:(idx % 2 + 1) * TCQ],
                                  pb[idx % 2][:, 0:TCQ]).then_inc(S["v2a"], 1)
                    v.wait_ge(S["t2g"], idx + 1)
                    if sidx >= 4:
                        v.wait_ge(S[f"w2s{sidx % 4}"], 16 * (sidx // 4))
                    v.tensor_scalar(
                        stage[:, (sidx % 4) * 512:(sidx % 4) * 512 + TCQ],
                        pb[2 + idx % 2][:, 0:TCQ], bqeff[:, cb:cb + 1], None,
                        op0=ALU.add).then_inc(S["e2"], 1)
                    idx += 1
                    sidx += 1
            for cb in range(ncbl):
                for tc in range(NTCK):
                    v.wait_ge(S["t2"], idx + 1)
                    if sidx >= 4:
                        v.wait_ge(S[f"w2s{sidx % 4}"], 16 * (sidx // 4))
                    v.tensor_scalar(
                        stage[:, (sidx % 4) * 512:(sidx % 4) * 512 + TCK],
                        pb[idx % 2][:, 0:TCK], bkp_s[:, cb:cb + 1], None,
                        op0=ALU.add).then_inc(S["e2"], 1)
                    idx += 1
                    sidx += 1
            for ch in range(NVCHL):
                for tb in range(ntb):
                    v.wait_ge(S["t2"], idx + 1)
                    if sidx >= 4:
                        v.wait_ge(S[f"w2s{sidx % 4}"], 16 * (sidx // 4))
                    v.tensor_copy(
                        stage[:, (sidx % 4) * 512:(sidx % 4) * 512 + VCH],
                        pb[idx % 2][:, 0:VCH]).then_inc(S["e2"], 1)
                    idx += 1
                    sidx += 1
            v.sem_inc(S["bar"], 1)
            v.wait_ge(S["bar"], 15)
            v.wait_ge(S["t2"], E2T)
            for hb in range(2):
                vv = v_sb[:, hb * ntb * VW:(hb + 1) * ntb * VW] \
                    .rearrange("p (m c) -> p m c", m=ntb)
                v.memset(vv[:, :, HCH:HCH + 1], 1.0).then_inc(S["m3"], 1)
            qcg = -1
            for vh, h, qcs in ATT_VH:
                for qc in qcs:
                    qcg += 1
                    for qb in range(QBC):
                        og = qcg * QBC + qb
                        v.wait_ge(S["t3o"], og + 1)
                        if qcg >= 1:
                            v.wait_ge(S["s3n"], (qcg - 1) * QBC + qb + 1)
                        v.reciprocal(recip[:, qb:qb + 1],
                                     pb[2 + qb][:, HCH:HCH + 1]
                                     ).then_inc(S["v3r"], 1)
            v.sem_inc(S["bar"], 1)
            for ps in range(2):
                base = ps * FIP
                fpos = 0
                for kind, a1, a2 in SCHED:
                    if kind == "chain":
                        tb = a1
                        gtb = ps * ntbf + tb
                        g2 = gtb % 2
                        v.wait_ge(S["x4l"], 48 * (gtb + 1))
                        if gtb >= 1:
                            v.wait_ge(S["v4n"], gtb)
                            v.wait_ge(S["s4a"], 2 * gtb)
                        v.tensor_tensor(xmid[:], x4_tile[:], xa4_tile[:], op=ALU.add
                                        ).then_inc(S["v4m"], 1)
                        v.wait_ge(S["s4a"], 2 * (gtb + 1))
                        v.tensor_scalar(mv[:, 4 + 2 * g2:5 + 2 * g2],
                                        st4[:, 4 * g2:4 * g2 + 1], 1.0 / D, None,
                                        op0=ALU.mult).then_inc(S["v4a"], 1)
                        v.wait_ge(S["v4a"], 3 * gtb + 1)
                        v.tensor_scalar(st4[:, 4 * g2 + 2:4 * g2 + 3],
                                        mv[:, 4 + 2 * g2:5 + 2 * g2],
                                        mv[:, 4 + 2 * g2:5 + 2 * g2], None,
                                        op0=ALU.mult).then_inc(S["v4a"], 1)
                        v.wait_ge(S["v4a"], 3 * gtb + 2)
                        v.scalar_tensor_tensor(mv[:, 5 + 2 * g2:6 + 2 * g2],
                                               st4[:, 4 * g2 + 1:4 * g2 + 2], 1.0 / D,
                                               st4[:, 4 * g2 + 2:4 * g2 + 3],
                                               op0=ALU.mult,
                                               op1=ALU.subtract).then_inc(S["v4a"], 1)
                        v.wait_ge(S["s4q"], gtb + 1)
                        v.reciprocal(rstd[:, 2 + g2:3 + g2], stdv[:, 2 + g2:3 + g2]
                                     ).then_inc(S["v4r"], 1)
                        v.wait_ge(S["v4r"], gtb + 1)
                        if gtb >= 2:
                            v.wait_ge(S["t4t"], (gtb - 1) * NG)
                        v.tensor_scalar(xn2_tiles[g2][:], xmid[:],
                                        mv[:, 4 + 2 * g2:5 + 2 * g2],
                                        rstd[:, 2 + g2:3 + g2],
                                        op0=ALU.subtract, op1=ALU.mult
                                        ).then_inc(S["v4n"], 1)
                        v.wait_ge(S["t4r"], gtb + 1)
                        if gtb >= 1:
                            v.wait_ge(S["s4g"], gtb - 1)
                        cb_ = gtb * CPT
                        sA = slice(g2 * NE, (g2 + 1) * NE)
                        v.reduce_max(m1s[:, 4 * g2:4 * g2 + 1], pb[2][:, 0:NE], axis=AX.X
                                     ).then_inc(S["v4c"], 1)
                        v.wait_ge(S["v4c"], cb_ + 1)
                        v.tensor_scalar(is1s[:, sA], pb[2][:, 0:NE],
                                        m1s[:, 4 * g2:4 * g2 + 1], None,
                                        op0=ALU.is_ge).then_inc(S["v4c"], 1)
                        v.wait_ge(S["v4c"], cb_ + 2)
                        v.scalar_tensor_tensor(r2s[:, sA], is1s[:, sA], -1e30,
                                               pb[2][:, 0:NE],
                                               op0=ALU.mult, op1=ALU.add
                                               ).then_inc(S["v4c"], 1)
                        v.wait_ge(S["v4c"], cb_ + 3)
                        v.reduce_max(m1s[:, 4 * g2 + 1:4 * g2 + 2], r2s[:, sA], axis=AX.X
                                     ).then_inc(S["v4c"], 1)
                        v.wait_ge(S["v4c"], cb_ + 4)
                        v.tensor_tensor(m1s[:, 4 * g2 + 2:4 * g2 + 3],
                                        m1s[:, 4 * g2:4 * g2 + 1],
                                        m1s[:, 4 * g2 + 1:4 * g2 + 2],
                                        op=ALU.subtract).then_inc(S["v4c"], 1)
                        v.wait_ge(S["v4c"], cb_ + 5)
                        v.tensor_scalar(is2s[:, sA], r2s[:, sA],
                                        m1s[:, 4 * g2 + 1:4 * g2 + 2], None,
                                        op0=ALU.is_ge).then_inc(S["v4c"], 1)
                        v.wait_ge(S["s4g"], gtb + 1)
                        v.tensor_scalar(w12[:, 2 * g2 + 1:2 * g2 + 2],
                                        w12[:, 2 * g2:2 * g2 + 1], -1.0, 1.0,
                                        op0=ALU.mult, op1=ALU.add).then_inc(S["v4c"], 1)
                        v.wait_ge(S["v4c"], cb_ + 7)
                        v.tensor_scalar(is1s[:, sA], is1s[:, sA],
                                        w12[:, 2 * g2:2 * g2 + 1], None,
                                        op0=ALU.mult).then_inc(S["v4c"], 1)
                        v.wait_ge(S["v4c"], cb_ + 8)
                        if gtb >= 1:
                            v.wait_ge(S["t4c"], gtb)
                        v.scalar_tensor_tensor(comb[:, g2 * NE:(g2 + 1) * NE],
                                               is2s[:, sA],
                                               w12[:, 2 * g2 + 1:2 * g2 + 2],
                                               is1s[:, sA],
                                               op0=ALU.mult, op1=ALU.add
                                               ).then_inc(S["v4c"], 1)
                        v.wait_ge(S["t4c"], gtb + 1)
                        v.tensor_copy(combT[0:NE, tb * 128:(tb + 1) * 128],
                                      pb[3][0:NE, 0:128]).then_inc(S["v4ct"], 1)
                    elif kind == "chunk":
                        c = a1
                        ce = ps * NQCF + c
                        v.wait_ge(S["t4x"], ce + 1)
                        v.tensor_copy(combexp[:, c * TCF:(c + 1) * TCF],
                                      pb[4][:, 0:TCF]).then_inc(S["v4x"], 1)
                        v.wait_ge(S["t4h"], ce + 1)
                        v.wait_ge(S["v4x"], ce + 1)
                        v.tensor_tensor(hwT[:, c * TCF:(c + 1) * TCF],
                                        pb[5][:, 0:TCF],
                                        combexp[:, c * TCF:(c + 1) * TCF],
                                        op=ALU.mult).then_inc(S["v4h"], 1)
                    else:
                        och, tb = a1, a2
                        fidx = base + fpos
                        v.wait_ge(S["t4f"], fidx + 1)
                        v.wait_ge(S[f"a4c{fidx % 4}"], 16 * (fidx // 4 + 1))
                        if fidx >= 2:
                            v.wait_ge(S["g4f"], fidx - 1)
                        v.tensor_tensor(t1_st[:, (fidx % 2) * OCH:(fidx % 2 + 1) * OCH],
                                        pb[fidx % 2][:, 0:OCH],
                                        xa_sl[:, (fidx % 4) * OCH:(fidx % 4 + 1) * OCH],
                                        op=ALU.add).then_inc(S["v4f"], 1)
                        fpos += 1

        @block.scalar
        def _(sc):
            sc.sem_inc(S["bar"], 1)
            sc.wait_ge(S["bar"], 5)
            for tb in range(ntb):
                sc.wait_ge(S["v1a"], 2 * (tb + 1))
                sc.activation(stdv[:, 0:1], mv[:, 1:2], AF.Sqrt, bias=ceps_s[:, 0:1]
                              ).then_inc(S["s1q"], 1)
                for gp in range(NG):
                    gi = tb * NG + gp
                    sc.wait_ge(S["t1"], gi + 1)
                    sc.activation(
                        xn1T[:, gp * 4:gp * 4 + 4, tb * 128:(tb + 1) * 128],
                        pb[gi % 2][:].rearrange("p (a b) -> p a b", a=4),
                        AF.Copy).then_inc(S["s1e"], 1)
            sc.sem_inc(S["bar"], 1)
            sc.wait_ge(S["bar"], 10)
            sc.sem_inc(S["bar"], 1)
            sc.wait_ge(S["bar"], 15)
            qcg = -1
            sprev = None

            def s_norms(oqcg, oqc):
                for qb in range(QBC):
                    og = oqcg * QBC + qb
                    sc.wait_ge(S["v3r"], og + 1)
                    sl = og % 4
                    if og >= 4:
                        sc.wait_ge(S[f"w3s{sl}"], 16 * (og // 4))
                    sc.activation(
                        xstg[:, sl * HCH:(sl + 1) * HCH],
                        pb[2 + qb][:, 0:HCH], AF.Copy,
                        scale=recip[:, qb:qb + 1]).then_inc(S["s3n"], 1)
                    sc.wait_ge(S["s3n"], og + 1)
                    tb = oqc * QBC + qb
                    sc.dma_start(
                        xap[tb, :, h_of[oqcg] * HCH:(h_of[oqcg] + 1) * HCH],
                        xstg[:, sl * HCH:(sl + 1) * HCH]
                    ).then_inc(S[f"w3s{sl}"], 16)

            h_of = {}
            for vh, h, qcs in ATT_VH:
                for qc in qcs:
                    qcg += 1
                    h_of[qcg] = h
                    for mb in range(ntb):
                        li = qcg * ntb + mb
                        sc.wait_ge(S["t3l"], li + 1)
                        if qcg >= 2:
                            sc.wait_ge(S["t3o"], QBC * (qcg - 1))
                        sc.activation(
                            attnT[:, (qcg % 2) * ntb * TCQ + mb * TCQ:
                                  (qcg % 2) * ntb * TCQ + (mb + 1) * TCQ],
                            pb[(0, 1, 6, 7)[li % 4]][:, 0:TCQ], AF.Exp,
                            bias=cshift_s[:, 0:1]).then_inc(S["s3e"], 1)
                    if sprev is not None:
                        s_norms(*sprev)
                    sprev = (qcg, qc)
            s_norms(*sprev)
            sc.sem_inc(S["bar"], 1)
            for j in range(4):
                sc.wait_ge(S[f"w3s{j}"],
                           16 * (TOG // 4 + (1 if TOG % 4 > j else 0)))
            for ps in range(2):
                base = ps * FIP
                fpos = 0
                for kind, a1, a2 in SCHED:
                    if kind == "chain":
                        tb = a1
                        gtb = ps * ntbf + tb
                        g2 = gtb % 2
                        sc.wait_ge(S["v4m"], gtb + 1)
                        if gtb >= 2:
                            sc.wait_ge(S["v4a"], 3 * (gtb - 1))
                        sc.activation(xa4_tile[:], xmid[:], AF.Copy,
                                      accum_out=st4[:, 4 * g2:4 * g2 + 1]
                                      ).then_inc(S["s4a"], 1)
                        sc.wait_ge(S["s4a"], 2 * gtb + 1)
                        sc.activation(xa4_tile[:], xmid[:], AF.Square,
                                      accum_out=st4[:, 4 * g2 + 1:4 * g2 + 2]
                                      ).then_inc(S["s4a"], 1)
                        sc.wait_ge(S["v4a"], 3 * (gtb + 1))
                        if gtb >= 2:
                            sc.wait_ge(S["v4r"], gtb - 1)
                        sc.activation(stdv[:, 2 + g2:3 + g2],
                                      mv[:, 5 + 2 * g2:6 + 2 * g2],
                                      AF.Sqrt, bias=ceps_s[:, 0:1]).then_inc(S["s4q"], 1)
                        if ps == 1 and tb == 0:
                            sc.wait_ge(S["t4f"], FIP)
                        for gp in range(NG):
                            gi = gtb * NG + gp
                            sc.wait_ge(S["t4t"], gi + 1)
                            sc.activation(
                                xn2T[:, gp * 4:gp * 4 + 4, tb * 128:(tb + 1) * 128],
                                pb[6 + gi % 2][:].rearrange("p (a b) -> p a b", a=4),
                                AF.Copy).then_inc(S["s4e"], 1)
                        sc.wait_ge(S["v4c"], gtb * CPT + 5)
                        sc.activation(w12[:, 2 * g2:2 * g2 + 1],
                                      m1s[:, 4 * g2 + 2:4 * g2 + 3], AF.Sigmoid
                                      ).then_inc(S["s4g"], 1)
                    elif kind == "ffn":
                        och, tb = a1, a2
                        fidx = base + fpos
                        gtb = ps * ntbf + tb
                        sc.wait_ge(S["g4f"], fidx + 1)
                        sc.dma_start(
                            y[gtb * 128:(gtb + 1) * 128, och * OCH:(och + 1) * OCH],
                            out_st[:, (fidx % 2) * OCH:(fidx % 2 + 1) * OCH]
                        ).then_inc(S[f"y4s{fidx % 2}"], 16)
                        fpos += 1
            for j in range(2):
                sc.wait_ge(S[f"y4s{j}"], 16 * (FI // 2 + (1 if FI % 2 > j else 0)))

    return nc


def host_layout(cfg, p, Wq, Wk, Wv, W_base, bq, bk, bv, b_base, bias_weight,
                W_router, A, B):
    D, CW, OW = cfg["D"], cfg["CW"], cfg["OW"]
    nk, ncbl = cfg["nk"], cfg["ncbl"]
    NVCHL, VCH, NOCH, OCH = cfg["NVCHL"], cfg["VCH"], cfg["NOCH"], cfg["OCH"]
    NHL = cfg["NHL"]
    chs = slice(p * CW, (p + 1) * CW)
    osl = slice(p * OW, (p + 1) * OW)

    def tile_wt(WT, nch, chw):
        return np.ascontiguousarray(
            WT.reshape(nk, 128, nch, chw).transpose(2, 1, 0, 3))

    out = {}
    out["wqt"] = tile_wt(np.ascontiguousarray(Wq.T[:, chs]), ncbl, 128)
    out["wkt"] = tile_wt(np.ascontiguousarray(Wk.T[:, chs]), ncbl, 128)
    out["wvt"] = tile_wt(np.ascontiguousarray(Wv.T[:, chs]), NVCHL, VCH)
    out["wbt"] = tile_wt(np.ascontiguousarray(W_base.T[:, osl]), NOCH, OCH)
    A_allT = np.ascontiguousarray(A.reshape(cfg["NE"] * cfg["R"], D).T)
    out["at"] = np.ascontiguousarray(
        A_allT.reshape(nk, 128, 128).transpose(1, 0, 2))
    out["bt"] = np.ascontiguousarray(
        B.transpose(0, 2, 1).reshape(cfg["NE"] * cfg["R"], D)[:, osl])
    WrT = np.ascontiguousarray(W_router.T)
    out["wrt"] = np.ascontiguousarray(
        WrT.reshape(nk, 128, cfg["NE"]).transpose(1, 0, 2))
    out["bqp"] = np.ascontiguousarray(bq[chs].reshape(ncbl, 128).T)
    out["bkp"] = np.ascontiguousarray(bk[chs].reshape(ncbl, 128).T)
    bvz = np.zeros((128, CW), dtype=np.float32)
    bvz[0] = bv[chs]
    out["bvr"] = bvz
    bbz = np.zeros((128, OW), dtype=np.float32)
    bbz[0] = b_base[osl]
    out["bbr"] = bbz
    out["bwt"] = np.ascontiguousarray(bias_weight[p * NHL:(p + 1) * NHL].T)
    out.update(host_consts(cfg))
    return out


def kernel(L0_attn_x, L0_attn_Wq, L0_attn_bq, L0_attn_Wk, L0_attn_bk,
           L0_attn_Wv, L0_attn_bv, L0_attn_bias_weight, L0_ffn_W_base,
           L0_ffn_b_base, L0_ffn_W_router, L0_ffn_A, L0_ffn_B):
    cfg = make_cfg(D=4096, L=2048, NH=16, CD=32)
    OW = cfg["OW"]
    x = np.asarray(L0_attn_x)
    Bz, L, D = x.shape

    args = (np.asarray(L0_attn_Wq), np.asarray(L0_attn_Wk),
            np.asarray(L0_attn_Wv), np.asarray(L0_ffn_W_base),
            np.asarray(L0_attn_bq), np.asarray(L0_attn_bk),
            np.asarray(L0_attn_bv), np.asarray(L0_ffn_b_base),
            np.asarray(L0_attn_bias_weight), np.asarray(L0_ffn_W_router),
            np.asarray(L0_ffn_A), np.asarray(L0_ffn_B))
    shared = [host_layout(cfg, p, *args) for p in range(2)]

    in_maps = []
    for c in range(8):
        b, p = c // 2, c % 2
        m = dict(shared[p])
        m["xc"] = np.ascontiguousarray(x[b])
        m["xres"] = np.ascontiguousarray(x[b][:, p * OW:(p + 1) * OW])
        in_maps.append(m)

    nc = build_program(cfg)
    res = run_bass_kernel_spmd(nc, in_maps, core_ids=list(range(8)))

    out = np.empty((Bz, L, D), dtype=np.float32)
    for c in range(8):
        b, p = c // 2, c % 2
        out[b, :, p * OW:(p + 1) * OW] = res.results[c]["y"]
    return out


# version 13
# speedup vs baseline: 1.1627x; 1.0801x over previous
import numpy as np
import ml_dtypes
from contextlib import ExitStack

import concourse.bass as bass
import concourse.mybir as mybir
from concourse.bass_utils import run_bass_kernel_spmd

F32 = mybir.dt.float32
BF16 = mybir.dt.bfloat16
U8 = mybir.dt.uint8
AF = mybir.ActivationFunctionType
ALU = mybir.AluOpType
AX = mybir.AxisListType

BF = ml_dtypes.bfloat16

SHIFT = 18.0
EPS = 1e-5


def _build_tables(g=(1.0, 1.0, 1.0)):
    dim = len(g)
    n = 2 ** dim
    masks = sorted(range(n), key=lambda m: (bin(m).count('1'), m))
    inv = {m: i for i, m in enumerate(masks)}
    C = np.zeros((n, n, n), dtype=np.float32)
    for i, a in enumerate(masks):
        for j, b in enumerate(masks):
            sw, at = (0, a >> 1)
            while at:
                sw += bin(at & b).count('1')
                at >>= 1
            s = -1.0 if sw & 1 else 1.0
            for t in range(dim):
                if a >> t & 1 and b >> t & 1:
                    s *= float(g[t])
            C[i, j, inv[a ^ b]] = s
    grades = np.array([bin(m).count('1') for m in masks])
    rev = np.where(grades * (grades - 1) // 2 % 2 == 1, -1.0, 1.0).astype(np.float32)
    met = np.array([float(np.prod([g[t] for t in range(dim) if m >> t & 1] + [1.0]))
                    for m in masks], dtype=np.float32)
    return C, rev, met


def make_cfg(D=4096, L=2048, NH=16, CD=32):
    NB = 8
    cfg = dict(D=D, L=L, NH=NH, CD=CD, NB=NB, NE=8, R=16, SCALING=2.0)
    cfg["HCH"] = CD * NB
    cfg["CBH"] = cfg["HCH"] // 128
    cfg["VW"] = cfg["HCH"] + 1
    cfg["NHL"] = NH // 2
    cfg["CW"] = cfg["NHL"] * cfg["HCH"]
    cfg["ncbl"] = cfg["CW"] // 128
    cfg["nk"] = D // 128
    cfg["ntb"] = L // 128
    cfg["TCQ"] = min(512, L)
    cfg["NQC"] = L // cfg["TCQ"]
    cfg["QBC"] = cfg["TCQ"] // 128
    cfg["TCK"] = min(512, L)
    cfg["NTCK"] = L // cfg["TCK"]
    cfg["VCH"] = min(256, cfg["CW"])
    cfg["NVCHL"] = cfg["CW"] // cfg["VCH"]
    cfg["OW"] = D // 2
    cfg["OCH"] = min(256, cfg["OW"])
    cfg["NOCH"] = cfg["OW"] // cfg["OCH"]
    cfg["LF"] = L // 2
    cfg["ntbf"] = cfg["LF"] // 128
    cfg["TCF"] = min(256, cfg["LF"])
    cfg["NQCF"] = cfg["LF"] // cfg["TCF"]
    cfg["NBS"] = max(1, D // 512)
    return cfg


def host_consts(cfg):
    C, rev, met = _build_tables()
    NHL, NB = cfg["NHL"], cfg["NB"]
    scale = 1.0 / np.sqrt(cfg["CD"] * NB)
    biv = C[:, :, [4, 5, 6]]
    bkt3 = np.ascontiguousarray(biv.transpose(2, 0, 1).reshape(3, NB * NB)).astype(np.float32)
    revd = np.tile(np.repeat(rev, NB)[None, :], (NHL, 1)).astype(np.float32)
    d1row = np.zeros(NB * NB, dtype=np.float32)
    for i in range(NB):
        d1row[i * NB + i] = rev[i] * scale * met[i]
    d1 = np.tile(d1row[None, :], (NHL, 1)).astype(np.float32)
    mask = np.zeros((128, 128), dtype=np.float32)
    for a in range(128):
        mask[a, (a // NB) * NB:(a // NB) * NB + NB] = 1.0
    rep = np.zeros((NB, 128), dtype=np.float32)
    for a in range(128):
        rep[a % NB, a] = 1.0
    exp8 = np.zeros((cfg["NE"], 128), dtype=np.float32)
    for er in range(cfg["NE"] * cfg["R"]):
        exp8[er // cfg["R"], er] = cfg["SCALING"]
    return dict(
        ibf=np.eye(128, dtype=np.float32).astype(BF),
        if32=np.eye(128, dtype=np.float32),
        rep=rep, mask=mask, exp8=exp8,
        row0=np.concatenate([np.ones((1, 128), dtype=np.float32),
                             np.zeros((127, 128), dtype=np.float32)]).astype(BF),
        bkt3=bkt3, revd=revd, d1=d1,
        ceps=np.full((128, 1), EPS, dtype=np.float32),
        cshift=np.full((128, 1), -SHIFT, dtype=np.float32),
    )


def build_program(cfg):
    D, L = cfg["D"], cfg["L"]
    NHL, HCH, CBH, VW = cfg["NHL"], cfg["HCH"], cfg["CBH"], cfg["VW"]
    NB, NE, CW = cfg["NB"], cfg["NE"], cfg["CW"]
    nk, ncbl, ntb = cfg["nk"], cfg["ncbl"], cfg["ntb"]
    TCQ, NQC, QBC = cfg["TCQ"], cfg["NQC"], cfg["QBC"]
    TCK, NTCK = cfg["TCK"], cfg["NTCK"]
    VCH, NVCHL = cfg["VCH"], cfg["NVCHL"]
    OW, OCH, NOCH = cfg["OW"], cfg["OCH"], cfg["NOCH"]
    LF, ntbf, TCF, NQCF = cfg["LF"], cfg["ntbf"], cfg["TCF"], cfg["NQCF"]
    NBS = cfg["NBS"]
    NG = nk // 4

    nc = bass.Bass("TRN2", target_bir_lowering=False)

    xc = nc.dram_tensor("xc", [L, D], F32, kind="ExternalInput").ap()
    xres = nc.dram_tensor("xres", [L, OW], F32, kind="ExternalInput").ap()
    wqt = nc.dram_tensor("wqt", [ncbl, 128, nk, 128], F32, kind="ExternalInput").ap()
    wkt = nc.dram_tensor("wkt", [ncbl, 128, nk, 128], F32, kind="ExternalInput").ap()
    wvt = nc.dram_tensor("wvt", [NVCHL, 128, nk, VCH], F32, kind="ExternalInput").ap()
    wbt = nc.dram_tensor("wbt", [NOCH, 128, nk, OCH], F32, kind="ExternalInput").ap()
    at_d = nc.dram_tensor("at", [128, nk, 128], F32, kind="ExternalInput").ap()
    bt_d = nc.dram_tensor("bt", [128, OW], F32, kind="ExternalInput").ap()
    wrt = nc.dram_tensor("wrt", [128, nk, NE], F32, kind="ExternalInput").ap()
    bqp = nc.dram_tensor("bqp", [128, ncbl], F32, kind="ExternalInput").ap()
    bkp = nc.dram_tensor("bkp", [128, ncbl], F32, kind="ExternalInput").ap()
    bvr = nc.dram_tensor("bvr", [128, CW], F32, kind="ExternalInput").ap()
    bbr = nc.dram_tensor("bbr", [128, OW], F32, kind="ExternalInput").ap()
    bwt_d = nc.dram_tensor("bwt", [3, NHL], F32, kind="ExternalInput").ap()
    ibf_d = nc.dram_tensor("ibf", [128, 128], BF16, kind="ExternalInput").ap()
    if32_d = nc.dram_tensor("if32", [128, 128], F32, kind="ExternalInput").ap()
    rep_d = nc.dram_tensor("rep", [NB, 128], F32, kind="ExternalInput").ap()
    mask_d = nc.dram_tensor("mask", [128, 128], F32, kind="ExternalInput").ap()
    exp8_d = nc.dram_tensor("exp8", [NE, 128], F32, kind="ExternalInput").ap()
    row0_d = nc.dram_tensor("row0", [128, 128], BF16, kind="ExternalInput").ap()
    bkt3_d = nc.dram_tensor("bkt3", [3, NB * NB], F32, kind="ExternalInput").ap()
    revd_d = nc.dram_tensor("revd", [NHL, NB * NB], F32, kind="ExternalInput").ap()
    d1_d = nc.dram_tensor("d1", [NHL, NB * NB], F32, kind="ExternalInput").ap()
    ceps_d = nc.dram_tensor("ceps", [128, 1], F32, kind="ExternalInput").ap()
    cshift_d = nc.dram_tensor("cshift", [128, 1], F32, kind="ExternalInput").ap()

    y = nc.dram_tensor("y", [L, OW], F32, kind="ExternalOutput").ap()
    qts = nc.dram_tensor("qts", [ncbl, 128, L], BF16, kind="Internal").ap()
    kts = nc.dram_tensor("kts", [ncbl, 128, L], BF16, kind="Internal").ap()
    vs = nc.dram_tensor("vs", [ntb, 128, CW], BF16, kind="Internal").ap()
    xap = nc.dram_tensor("xap", [ntb, 128, CW], BF16, kind="Internal").ap()
    xag1 = nc.dram_tensor("xag1", [2 * ntbf * 128, CW], BF16, kind="Internal").ap()
    xag2 = nc.dram_tensor("xag2", [2 * ntbf * 128, CW], BF16, kind="Internal").ap()
    talld = nc.dram_tensor("talld", [NHL, NB * NB], F32, kind="Internal").ap()

    es = ExitStack()
    with es:
        def sb(name, shape, dt):
            return es.enter_context(nc.sbuf_tensor(name, shape, dt))

        ibf = sb("ibf_s", [128, 128], BF16)
        if32 = sb("if32_s", [128, 128], F32)
        rep_s = sb("rep_s", [NB, 128], F32)
        mask_s = sb("mask_s", [128, 128], F32)
        exp8_s = sb("exp8_s", [NE, 128], F32)
        row0_s = sb("row0_s", [128, 128], BF16)
        bkt3 = sb("bkt3_s", [3, NB * NB], F32)
        revd = sb("revd_s", [NHL, NB * NB], F32)
        d1s = sb("d1_s", [NHL, NB * NB], F32)
        bwt_s = sb("bwt_s", [3, NHL], F32)
        ceps_s = sb("ceps_s", [128, 1], F32)
        cshift_s = sb("cshift_s", [128, 1], F32)
        G_sb = sb("G_sb", [128, NHL * 128], BF16)
        bkp_s = sb("bkp_s", [128, ncbl], F32)
        bqbf = sb("bqbf", [128, ncbl], BF16)
        bqeff = sb("bqeff", [128, ncbl], F32)
        ttile = sb("ttile", [NB, NHL * NB], F32)
        tmp64 = sb("tmp64", [NHL, NB * NB], F32)
        tall = sb("tall", [NHL, NB * NB], F32)
        trow = sb("trow", [128, 2 * NB], F32)
        stats = sb("stats", [128, 6 * NBS], F32)
        mv = sb("mv", [128, 8], F32)
        stdv = sb("stdv", [128, 4], F32)
        rstd = sb("rstd", [128, 4], F32)
        recip = sb("recip", [128, QBC], F32)
        comb = sb("comb", [128, 2 * NE], F32)
        m1s = sb("m1s", [128, 8], F32)
        st4 = sb("st4", [128, 8], F32)
        w12 = sb("w12", [128, 4], F32)
        r2s = sb("r2s", [128, 2 * NE], F32)
        is1s = sb("is1s", [128, 2 * NE], F32)
        is2s = sb("is2s", [128, 2 * NE], F32)

        AB = 192 * 1024
        arena = sb("arena", [128, AB], U8)

        def carve(off, nbytes, dt):
            off = (off + 31) // 32 * 32
            assert off + nbytes <= AB, (off, nbytes)
            return arena[:, off:off + nbytes].bitcast(dt), off + nbytes

        xn1T_f, o = carve(0, nk * L * 2, BF16)
        xn1T = xn1T_f.rearrange("p (k t) -> p k t", k=nk)
        PH = (o + 31) // 32 * 32
        o = PH
        x_tile0, o = carve(o, D * 4, F32)
        x_tile1, o = carve(o, D * 4, F32)
        xn_tile0, o = carve(o, D * 2, BF16)
        xn_tile1, o = carve(o, D * 2, BF16)
        assert o <= AB
        x_tiles = (x_tile0, x_tile1)
        xn_tiles = (xn_tile0, xn_tile1)
        o = PH
        wq0, o = carve(o, nk * 128 * 2, BF16)
        wq1, o = carve(o, nk * 128 * 2, BF16)
        wv0, o = carve(o, nk * VCH * 2, BF16)
        wv1, o = carve(o, nk * VCH * 2, BF16)
        qraw, o = carve(o, 2 * TCQ * 2, BF16)
        stage, o = carve(o, 4 * 512 * 2, BF16)
        bvrbf, o = carve(o, CW * 2, BF16)
        assert o <= AB
        o = 0
        q_sb, o = carve(o, 2 * CBH * L * 2, BF16)
        k_sb, o = carve(o, 2 * CBH * L * 2, BF16)
        v_sb, o = carve(o, 2 * ntb * VW * 2, BF16)
        attnT, o = carve(o, 2 * ntb * TCQ * 2, BF16)
        xstg, o = carve(o, 4 * HCH * 2, BF16)
        PH3_TOP = o
        assert o <= AB
        o = 0
        xn2T_f, o = carve(o, nk * LF * 2, BF16)
        xn2T = xn2T_f.rearrange("p (k t) -> p k t", k=nk)
        wb0, o = carve(o, nk * OCH * 2, BF16)
        wb1, o = carve(o, nk * OCH * 2, BF16)
        a_sb, o = carve(o, nk * 128 * 2, BF16)
        b_sb, o = carve(o, OW * 2, BF16)
        wr_sb, o = carve(o, nk * NE * 2, BF16)
        X4_OFF = (o + 31) // 32 * 32
        x4_tile, o = carve(o, D * 4, F32)
        xa4_tile, o = carve(o, D * 2, BF16)
        xmid, o = carve(o, D * 4, F32)
        xn2_tile0, o = carve(o, D * 2, BF16)
        xn2_tile1, o = carve(o, D * 2, BF16)
        xn2_tiles = (xn2_tile0, xn2_tile1)
        hwT, o = carve(o, LF * 2, BF16)
        combexp, o = carve(o, LF * 4, F32)
        combT, o = carve(o, LF * 4, F32)
        bbrbf, o = carve(o, OW * 2, BF16)
        xa_sl, o = carve(o, 4 * OCH * 2, BF16)
        x_sl, o = carve(o, 4 * OCH * 4, F32)
        t1_st, o = carve(o, 2 * OCH * 4, F32)
        out_st, o = carve(o, 2 * OCH * 4, F32)
        assert o <= AB

        pb = [es.enter_context(nc.psum_tensor(f"pb{i}", [128, 512], F32))
              for i in range(8)]

        names = ("bar c_ld c_gl d0 t0 v0 m3 "
                 "x1le x1lo v1a s1q v1r v1n t1 v1e "
                 "w2le w2lo t2 v2a t2g e2 w2s0 w2s1 w2s2 w2s3 "
                 "q3le q3lo t3l s3e t3o v3r s3n w3s0 w3s1 w3s2 w3s3 ccs "
                 "x4l v4m s4a v4a s4q v4r v4n t4t v4e t4r v4c s4g t4c v4ct "
                 "t4x v4x t4h v4h g4a g4le g4lo t4f v4f s1e s4e "
                 "x4c0 x4c1 x4c2 x4c3 a4c0 a4c1 a4c2 a4c3 g4f y4s0 y4s1").split()
        S = {n: es.enter_context(nc.semaphore(n)) for n in names}
        block = es.enter_context(nc.Block())

        NC_LD = 14
        if NQC >= 2:
            _halves = [list(range(0, NQC // 2)), list(range(NQC // 2, NQC))]
        else:
            _halves = [list(range(NQC))]
        ATT_VH = []
        vhi = 0
        for hf in range(len(_halves)):
            for h in range(NHL):
                ATT_VH.append((vhi, h, _halves[hf]))
                vhi += 1
        NQCV = len(_halves[0])
        OGV = NQCV * QBC
        QI = ncbl * NQC
        KI = ncbl * NTCK
        VI = NVCHL * ntb
        E2T = QI + KI + VI
        LD3 = 2 * CBH + ntb
        OG = NQC * QBC
        TOG = NHL * OG
        CPT = 9
        FIP = NOCH * ntbf
        FI = 2 * FIP
        CHTB = TCF // 128
        SCHED = []
        pend = []
        for tb in range(ntbf):
            k = 3
            while pend and k > 0:
                och, tb2 = pend.pop(0)
                SCHED.append(("ffn", och, tb2))
                k -= 1
            SCHED.append(("chain", tb, 0))
            if (tb + 1) % CHTB == 0:
                c = tb // CHTB
                SCHED.append(("chunk", c, 0))
                for och in range(min(2, NOCH)):
                    for tb2 in range(c * CHTB, (c + 1) * CHTB):
                        pend.append((och, tb2))
        while pend:
            och, tb2 = pend.pop(0)
            SCHED.append(("ffn", och, tb2))
        for och in range(min(2, NOCH), NOCH):
            for tb in range(ntbf):
                SCHED.append(("ffn", och, tb))
        FFN_ORDER = [(och, tb) for kind, och, tb in SCHED if kind == "ffn"]
        assert len(FFN_ORDER) == FIP
        LASTIDX = {}
        for i, (och, tb) in enumerate(FFN_ORDER):
            LASTIDX[och] = i

        @block.sync
        def _(s):
            for dst, src in ((ibf, ibf_d), (if32, if32_d), (rep_s, rep_d),
                             (mask_s, mask_d), (exp8_s, exp8_d),
                             (row0_s, row0_d), (bkt3, bkt3_d), (revd, revd_d),
                             (d1s, d1_d), (bwt_s, bwt_d), (bkp_s, bkp),
                             (ceps_s, ceps_d), (cshift_s, cshift_d)):
                s.dma_start(dst[:], src).then_inc(S["c_ld"], 16)
            s.dma_start(m1s[:, 3:4], ceps_d).then_inc(S["c_ld"], 16)
            s.wait_ge(S["v0"], 3)
            s.dma_start(talld, tall[:]).then_inc(S["d0"], 16)
            s.wait_ge(S["d0"], 16)
            s.dma_start(ttile[:].rearrange("i (h j) -> i h j", h=NHL),
                        talld.rearrange("h (i j) -> i h j", i=NB)
                        ).then_inc(S["d0"], 16)
            s.wait_ge(S["d0"], 32)
            s.sem_inc(S["bar"], 1)
            s.wait_ge(S["bar"], 5)
            for tb in range(ntb):
                if tb >= 2:
                    s.wait_ge(S["v1n"], tb - 1)
                s.dma_start(x_tiles[tb % 2], xc[tb * 128:(tb + 1) * 128, :]
                            ).then_inc(S["x1le" if tb % 2 == 0 else "x1lo"], 16)
            s.wait_ge(S["s1e"], ntb * NG)
            s.sem_inc(S["bar"], 1)
            s.wait_ge(S["bar"], 10)
            sidx = 0
            for cb in range(ncbl):
                for tc in range(NQC):
                    s.wait_ge(S["e2"], sidx + 1)
                    s.dma_start(qts[cb, :, tc * TCQ:(tc + 1) * TCQ],
                                stage[:, (sidx % 4) * 512:(sidx % 4) * 512 + TCQ]
                                ).then_inc(S[f"w2s{sidx % 4}"], 16)
                    sidx += 1
            for cb in range(ncbl):
                for tc in range(NTCK):
                    s.wait_ge(S["e2"], sidx + 1)
                    s.dma_start(kts[cb, :, tc * TCK:(tc + 1) * TCK],
                                stage[:, (sidx % 4) * 512:(sidx % 4) * 512 + TCK]
                                ).then_inc(S[f"w2s{sidx % 4}"], 16)
                    sidx += 1
            for ch in range(NVCHL):
                for tb in range(ntb):
                    s.wait_ge(S["e2"], sidx + 1)
                    s.dma_start(vs[tb, :, ch * VCH:(ch + 1) * VCH],
                                stage[:, (sidx % 4) * 512:(sidx % 4) * 512 + VCH]
                                ).then_inc(S[f"w2s{sidx % 4}"], 16)
                    sidx += 1
            for j in range(4):
                s.wait_ge(S[f"w2s{j}"], 16 * (E2T // 4 + (1 if E2T % 4 > j else 0)))
            s.sem_inc(S["bar"], 1)
            s.wait_ge(S["bar"], 15)
            s.wait_ge(S["t2"], E2T)
            for vh, h, qcs in ATT_VH:
                hb = vh % 2
                if vh >= 2:
                    s.wait_ge(S["t3o"], OGV * (vh - 1))
                q3 = "q3le" if vh % 2 == 0 else "q3lo"
                for cc in range(CBH):
                    s.dma_start(
                        q_sb[:, hb * CBH * L + cc * L:hb * CBH * L + (cc + 1) * L],
                        qts[h * CBH + cc]).then_inc(S[q3], 16)
                    s.dma_start(
                        k_sb[:, hb * CBH * L + cc * L:hb * CBH * L + (cc + 1) * L],
                        kts[h * CBH + cc]).then_inc(S[q3], 16)
                for mb in range(ntb):
                    s.dma_start(
                        v_sb[:, hb * ntb * VW + mb * VW:
                             hb * ntb * VW + mb * VW + HCH],
                        vs[mb, :, h * HCH:(h + 1) * HCH],
                    ).then_inc(S[q3], 16)
            s.sem_inc(S["bar"], 1)
            if X4_OFF < PH3_TOP:
                s.wait_ge(S["t3o"], TOG)
                for j in range(4):
                    s.wait_ge(S[f"w3s{j}"],
                              16 * (TOG // 4 + (1 if TOG % 4 > j else 0)))
            for ps in range(2):
                s.wait_ge(S["ccs"], ps + 1)
                xagN = xag1 if ps == 0 else xag2
                base = ps * FIP
                fpos = 0
                for kind, a1, a2 in SCHED:
                    if kind == "chain":
                        tb = a1
                        gtb = ps * ntbf + tb
                        if gtb >= 1:
                            s.wait_ge(S["v4m"], gtb)
                            s.wait_ge(S["s4a"], 2 * gtb)
                        s.dma_start(x4_tile, xc[gtb * 128:(gtb + 1) * 128, :]
                                    ).then_inc(S["x4l"], 16)
                        s.dma_start(xa4_tile[:, 0:CW],
                                    xagN[tb * 128:(tb + 1) * 128, :]
                                    ).then_inc(S["x4l"], 16)
                        s.dma_start(xa4_tile[:, CW:D],
                                    xagN[(ntbf + tb) * 128:(ntbf + tb + 1) * 128, :]
                                    ).then_inc(S["x4l"], 16)
                    elif kind == "ffn":
                        och, tb = a1, a2
                        fidx = base + fpos
                        gtb = ps * ntbf + tb
                        if fidx >= 4:
                            s.wait_ge(S["v4f"], fidx - 3)
                            s.wait_ge(S["g4f"], fidx - 3)
                        s.dma_start(
                            xa_sl[:, (fidx % 4) * OCH:(fidx % 4 + 1) * OCH],
                            xap[gtb, :, och * OCH:(och + 1) * OCH]
                        ).then_inc(S[f"a4c{fidx % 4}"], 16)
                        s.dma_start(
                            x_sl[:, (fidx % 4) * OCH:(fidx % 4 + 1) * OCH],
                            xres[gtb * 128:(gtb + 1) * 128, och * OCH:(och + 1) * OCH]
                        ).then_inc(S[f"x4c{fidx % 4}"], 16)
                        fpos += 1

        @block.gpsimd
        def _(g):
            g.dma_start(bqbf[:], bqp).then_inc(S["c_gl"], 16)
            g.dma_start(bvrbf[:], bvr).then_inc(S["c_gl"], 16)
            g.sem_inc(S["bar"], 1)
            g.wait_ge(S["bar"], 5)
            g.sem_inc(S["bar"], 1)
            g.wait_ge(S["bar"], 10)
            g.wait_ge(S["v1n"], ntb)
            g.wait_ge(S["t1"], ntb * NG)
            for cb in range(ncbl):
                if cb >= 2:
                    g.wait_ge(S["t2"], NQC * (cb - 1))
                g.dma_start((wq0 if cb % 2 == 0 else wq1)[:],
                            wqt[cb].rearrange("p k c -> p (k c)")
                            ).then_inc(S["w2le" if cb % 2 == 0 else "w2lo"], 16)
            for cb in range(ncbl):
                if cb == 0:
                    g.wait_ge(S["t2"], (ncbl - 1) * NQC)
                elif cb == 1:
                    g.wait_ge(S["t2"], QI)
                else:
                    g.wait_ge(S["t2"], QI + NTCK * (cb - 1))
                g.dma_start((wq0 if cb % 2 == 0 else wq1)[:],
                            wkt[cb].rearrange("p k c -> p (k c)")
                            ).then_inc(S["w2le" if cb % 2 == 0 else "w2lo"], 16)
            for ch in range(NVCHL):
                if ch == 0:
                    g.wait_ge(S["w2le"], 16 * ncbl)
                elif ch == 1:
                    g.wait_ge(S["w2lo"], 16 * ncbl)
                else:
                    g.wait_ge(S["t2"], QI + KI + ntb * (ch - 1))
                g.dma_start((wv0 if ch % 2 == 0 else wv1)[:],
                            wvt[ch].rearrange("p k c -> p (k c)")
                            ).then_inc(S["w2le" if ch % 2 == 0 else "w2lo"], 16)
            g.sem_inc(S["bar"], 1)
            g.wait_ge(S["bar"], 15)
            g.sem_inc(S["bar"], 1)
            slot_seq = [[] for _ in range(4)]
            _qcg = 0
            for vh, h, qcs in ATT_VH:
                for qc in qcs:
                    for qb in range(QBC):
                        og = _qcg * QBC + qb
                        slot_seq[og % 4].append(qc * QBC + qb)
                    _qcg += 1
            for j in range(4):
                fh = [i for i, tb_ in enumerate(slot_seq[j]) if tb_ < ntbf]
                thr = (fh[-1] + 1) if fh else 0
                if thr:
                    g.wait_ge(S[f"w3s{j}"], 16 * thr)
            g.collective_compute(
                "AllGather", mybir.AluOpType.bypass,
                replica_groups=[[0, 1], [2, 3], [4, 5], [6, 7]],
                ins=[xap[0:ntbf].rearrange("m p c -> (m p) c").opt()],
                outs=[xag1.opt()],
            ).then_inc(S["ccs"], 1)
            g.wait_ge(S["t3o"], TOG)
            for j in range(4):
                g.wait_ge(S[f"w3s{j}"], 16 * len(slot_seq[j]))
            g.collective_compute(
                "AllGather", mybir.AluOpType.bypass,
                replica_groups=[[0, 1], [2, 3], [4, 5], [6, 7]],
                ins=[xap[ntbf:ntb].rearrange("m p c -> (m p) c").opt()],
                outs=[xag2.opt()],
            ).then_inc(S["ccs"], 1)
            g.dma_start(a_sb[:], at_d.rearrange("p k c -> p (k c)")).then_inc(S["g4a"], 16)
            g.dma_start(b_sb[:], bt_d).then_inc(S["g4a"], 16)
            g.dma_start(wr_sb[:], wrt.rearrange("p k c -> p (k c)")).then_inc(S["g4a"], 16)
            g.dma_start(bbrbf[:], bbr).then_inc(S["g4a"], 16)
            for ps in range(2):
                base = ps * FIP
                if ps == 0:
                    g.dma_start(wb0[:], wbt[0].rearrange("p k c -> p (k c)")
                                ).then_inc(S["g4le"], 16)
                    if NOCH > 1:
                        g.dma_start(wb1[:], wbt[1].rearrange("p k c -> p (k c)")
                                    ).then_inc(S["g4lo"], 16)
                else:
                    g.wait_ge(S["t4f"], LASTIDX[0] + 1)
                    g.dma_start(wb0[:], wbt[0].rearrange("p k c -> p (k c)")
                                ).then_inc(S["g4le"], 16)
                    if NOCH > 1:
                        g.wait_ge(S["t4f"], LASTIDX[1] + 1)
                        g.dma_start(wb1[:], wbt[1].rearrange("p k c -> p (k c)")
                                    ).then_inc(S["g4lo"], 16)
                for fpos, (och, tb) in enumerate(FFN_ORDER):
                    idx = base + fpos
                    g.wait_ge(S["v4f"], idx + 1)
                    g.wait_ge(S[f"x4c{idx % 4}"], 16 * (idx // 4 + 1))
                    if idx >= 2:
                        g.wait_ge(S[f"y4s{idx % 2}"], 16 * (idx // 2))
                    g.tensor_add(
                        out_st[:, (idx % 2) * OCH:(idx % 2 + 1) * OCH],
                        t1_st[:, (idx % 2) * OCH:(idx % 2 + 1) * OCH],
                        x_sl[:, (idx % 4) * OCH:(idx % 4 + 1) * OCH],
                    ).then_inc(S["g4f"], 1)
                    for pf in range(NOCH):
                        if pf + 2 < NOCH and LASTIDX[pf] == fpos:
                            g.wait_ge(S["t4f"], base + LASTIDX[pf] + 1)
                            g.dma_start(
                                (wb0 if (pf + 2) % 2 == 0 else wb1)[:],
                                wbt[pf + 2].rearrange("p k c -> p (k c)")
                                ).then_inc(S["g4le" if (pf + 2) % 2 == 0 else "g4lo"], 16)

        @block.tensor
        def _(t):
            t.wait_ge(S["c_ld"], 16 * NC_LD)
            t.wait_ge(S["c_gl"], 32)
            t.matmul(pb[6][0:NHL, 0:NB * NB], bwt_s[:, 0:NHL], bkt3[:, :],
                     start=True, stop=True).then_inc(S["t0"], 1)
            t.wait_ge(S["d0"], 32)
            for h in range(NHL):
                t.wait_ge(S["v0"], 1 if h == 0 else 3 + 2 * (h - 1) + 1)
                t.matmul(pb[6][:, 256 + h * NB:256 + (h + 1) * NB], rep_s[:, :],
                         ttile[:, h * NB:(h + 1) * NB],
                         start=True, stop=True).then_inc(S["t0"], 1)
            t.wait_ge(S["v0"], 3 + 2 * NHL)
            for cb in range(ncbl):
                h = cb // CBH
                mm = t.matmul(pb[7][:, cb:cb + 1],
                              G_sb[:, h * 128:(h + 1) * 128],
                              bqbf[:, cb:cb + 1], start=True, stop=True)
            mm.then_inc(S["t0"], 1)
            t.sem_inc(S["bar"], 1)
            t.wait_ge(S["bar"], 5)
            gidx = 0
            for tb in range(ntb):
                t.wait_ge(S["v1n"], tb + 1)
                for gp in range(NG):
                    if gidx >= 2:
                        t.wait_ge(S["s1e"], gidx - 1)
                    for j in range(4):
                        k = gp * 4 + j
                        mm = t.matmul(pb[gidx % 2][:, j * 128:(j + 1) * 128],
                                      xn_tiles[tb % 2][:, k * 128:(k + 1) * 128],
                                      ibf[:], start=True, stop=True)
                    mm.then_inc(S["t1"], 1)
                    gidx += 1
            t.sem_inc(S["bar"], 1)
            t.wait_ge(S["bar"], 10)
            idx = 0
            for cb in range(ncbl):
                wbuf = (wq0 if cb % 2 == 0 else wq1)
                h = cb // CBH
                t.wait_ge(S["w2le" if cb % 2 == 0 else "w2lo"], 16 * (cb // 2 + 1))
                for tc in range(NQC):
                    if idx >= 2:
                        t.wait_ge(S["v2a"], idx - 1)
                    for k in range(nk):
                        mm = t.matmul(pb[idx % 2][:, 0:TCQ],
                                      wbuf[:, k * 128:(k + 1) * 128],
                                      xn1T[:, k, tc * TCQ:(tc + 1) * TCQ],
                                      start=(k == 0), stop=(k == nk - 1))
                    mm.then_inc(S["t2"], 1)
                    t.wait_ge(S["v2a"], idx + 1)
                    if idx >= 2:
                        t.wait_ge(S["e2"], idx - 1)
                    t.matmul(pb[2 + idx % 2][:, 0:TCQ],
                             G_sb[:, h * 128:(h + 1) * 128],
                             qraw[:, (idx % 2) * TCQ:(idx % 2 + 1) * TCQ],
                             start=True, stop=True).then_inc(S["t2g"], 1)
                    idx += 1
            for cb in range(ncbl):
                wbuf = (wq0 if cb % 2 == 0 else wq1)
                t.wait_ge(S["w2le" if cb % 2 == 0 else "w2lo"],
                          16 * (ncbl // 2 + cb // 2 + 1))
                for tc in range(NTCK):
                    if idx >= 2:
                        t.wait_ge(S["e2"], idx - 1)
                    for k in range(nk):
                        mm = t.matmul(pb[idx % 2][:, 0:TCK],
                                      wbuf[:, k * 128:(k + 1) * 128],
                                      xn1T[:, k, tc * TCK:(tc + 1) * TCK],
                                      start=(k == 0), stop=(k == nk - 1))
                    mm.then_inc(S["t2"], 1)
                    idx += 1
            for ch in range(NVCHL):
                vbuf = (wv0 if ch % 2 == 0 else wv1)
                t.wait_ge(S["w2le" if ch % 2 == 0 else "w2lo"],
                          16 * (ncbl + ch // 2 + 1))
                for tb in range(ntb):
                    if idx >= 2:
                        t.wait_ge(S["e2"], idx - 1)
                    for k in range(nk):
                        t.matmul(pb[idx % 2][:, 0:VCH],
                                 xn1T[:, k, tb * 128:(tb + 1) * 128],
                                 vbuf[:, k * VCH:(k + 1) * VCH],
                                 start=(k == 0), stop=False)
                    t.matmul(pb[idx % 2][:, 0:VCH], row0_s[:],
                             bvrbf[:, ch * VCH:(ch + 1) * VCH],
                             start=False, stop=True).then_inc(S["t2"], 1)
                    idx += 1
            t.sem_inc(S["bar"], 1)
            t.wait_ge(S["bar"], 15)
            t.wait_ge(S["m3"], 2)
            qcg = -1
            for vh, h, qcs in ATT_VH:
                hb = vh % 2
                t.wait_ge(S["q3le" if vh % 2 == 0 else "q3lo"],
                          16 * LD3 * (vh // 2 + 1))
                for qc in qcs:
                    qcg += 1
                    for mb in range(ntb):
                        li = qcg * ntb + mb
                        if li >= 4:
                            t.wait_ge(S["s3e"], li - 3)
                        for cc in range(CBH):
                            mm = t.matmul(
                                pb[(0, 1, 6, 7)[li % 4]][:, 0:TCQ],
                                k_sb[:, hb * CBH * L + cc * L + mb * 128:
                                     hb * CBH * L + cc * L + mb * 128 + 128],
                                q_sb[:, hb * CBH * L + cc * L + qc * TCQ:
                                     hb * CBH * L + cc * L + qc * TCQ + TCQ],
                                start=(cc == 0), stop=(cc == CBH - 1))
                        mm.then_inc(S["t3l"], 1)
                    t.wait_ge(S["s3e"], ntb * (qcg + 1))
                    for qb in range(QBC):
                        if qcg >= 1:
                            t.wait_ge(S["s3n"], (qcg - 1) * QBC + qb + 1)
                        for mb in range(ntb):
                            mm = t.matmul(
                                pb[2 + qb][:, 0:VW],
                                attnT[:, (qcg % 2) * ntb * TCQ + mb * TCQ + qb * 128:
                                      (qcg % 2) * ntb * TCQ + mb * TCQ + qb * 128 + 128],
                                v_sb[:, hb * ntb * VW + mb * VW:
                                     hb * ntb * VW + (mb + 1) * VW],
                                start=(mb == 0), stop=(mb == ntb - 1))
                        mm.then_inc(S["t3o"], 1)
            t.sem_inc(S["bar"], 1)
            t.wait_ge(S["bar"], 20)
            gidx = 0
            for ps in range(2):
                base = ps * FIP
                fpos = 0
                for kind, a1, a2 in SCHED:
                    if kind == "chain":
                        tb = a1
                        gtb = ps * ntbf + tb
                        t.wait_ge(S["v4n"], gtb + 1)
                        for gp in range(NG):
                            if gidx >= 2:
                                t.wait_ge(S["s4e"], gidx - 1)
                            for j in range(4):
                                k = gp * 4 + j
                                mm = t.matmul(pb[6 + gidx % 2][:, j * 128:(j + 1) * 128],
                                              xn2_tiles[gtb % 2][:, k * 128:(k + 1) * 128],
                                              ibf[:], start=True, stop=True)
                            mm.then_inc(S["t4t"], 1)
                            gidx += 1
                        t.wait_ge(S["s4e"], (gtb + 1) * NG)
                        t.wait_ge(S["g4a"], 16 * 4)
                        if gtb >= 1:
                            t.wait_ge(S["v4c"], (gtb - 1) * CPT + 3)
                        for k in range(nk):
                            mm = t.matmul(pb[2][:, 0:NE],
                                          xn2T[:, k, tb * 128:(tb + 1) * 128],
                                          wr_sb[:, k * NE:(k + 1) * NE],
                                          start=(k == 0), stop=(k == nk - 1))
                        mm.then_inc(S["t4r"], 1)
                        t.wait_ge(S["v4c"], gtb * CPT + CPT)
                        if gtb >= 1:
                            t.wait_ge(S["v4ct"], gtb)
                        t.matmul(pb[3][0:NE, 0:128],
                                 comb[:, (gtb % 2) * NE:(gtb % 2 + 1) * NE],
                                 if32[:], start=True, stop=True).then_inc(S["t4c"], 1)
                    elif kind == "chunk":
                        c = a1
                        ce = ps * NQCF + c
                        t.wait_ge(S["v4ct"], ps * ntbf + (c + 1) * CHTB)
                        if ce >= 1:
                            t.wait_ge(S["v4x"], ce)
                        t.matmul(pb[4][:, 0:TCF], exp8_s[:, :],
                                 combT[0:NE, c * TCF:(c + 1) * TCF],
                                 start=True, stop=True).then_inc(S["t4x"], 1)
                        if ce >= 1:
                            t.wait_ge(S["v4h"], ce)
                        for k in range(nk):
                            mm = t.matmul(pb[5][:, 0:TCF],
                                          a_sb[:, k * 128:(k + 1) * 128],
                                          xn2T[:, k, c * TCF:(c + 1) * TCF],
                                          start=(k == 0), stop=(k == nk - 1))
                        mm.then_inc(S["t4h"], 1)
                    else:
                        och, tb = a1, a2
                        fidx = base + fpos
                        if och % 2 == 0:
                            t.wait_ge(S["g4le"], 16 * (ps * ((NOCH + 1) // 2) + och // 2 + 1))
                        else:
                            t.wait_ge(S["g4lo"], 16 * (ps * (NOCH // 2) + och // 2 + 1))
                        t.wait_ge(S["s4e"], (ps * ntbf + tb + 1) * NG)
                        t.wait_ge(S["v4h"], ps * NQCF + tb // CHTB + 1)
                        if fidx >= 2:
                            t.wait_ge(S["v4f"], fidx - 1)
                        wbuf = (wb0 if och % 2 == 0 else wb1)
                        for k in range(nk):
                            t.matmul(pb[fidx % 2][:, 0:OCH],
                                     xn2T[:, k, tb * 128:(tb + 1) * 128],
                                     wbuf[:, k * OCH:(k + 1) * OCH],
                                     start=(k == 0), stop=False)
                        t.matmul(pb[fidx % 2][:, 0:OCH],
                                 hwT[:, tb * 128:(tb + 1) * 128],
                                 b_sb[:, och * OCH:(och + 1) * OCH],
                                 start=False, stop=False)
                        t.matmul(pb[fidx % 2][:, 0:OCH], row0_s[:],
                                 bbrbf[:, och * OCH:(och + 1) * OCH],
                                 start=False, stop=True).then_inc(S["t4f"], 1)
                        fpos += 1

        @block.vector
        def _(v):
            v.wait_ge(S["t0"], 1)
            v.tensor_tensor(tmp64[:], pb[6][0:NHL, 0:NB * NB], revd[:],
                            op=ALU.mult).then_inc(S["v0"], 1)
            v.wait_ge(S["v0"], 1)
            v.tensor_tensor(tall[:], tmp64[:], d1s[:], op=ALU.add
                            ).then_inc(S["v0"], 1)
            v.sem_inc(S["v0"], 1)
            for h in range(NHL):
                v.wait_ge(S["t0"], 2 + h)
                v.tensor_copy(trow[:, (h % 2) * NB:(h % 2 + 1) * NB],
                              pb[6][:, 256 + h * NB:256 + (h + 1) * NB]
                              ).then_inc(S["v0"], 1)
                v.wait_ge(S["v0"], 3 + 2 * h + 1)
                for rp in range(128 // NB):
                    mm = v.tensor_tensor(
                        G_sb[:, h * 128 + rp * NB:h * 128 + (rp + 1) * NB],
                        trow[:, (h % 2) * NB:(h % 2 + 1) * NB],
                        mask_s[:, rp * NB:(rp + 1) * NB], op=ALU.mult)
                mm.then_inc(S["v0"], 1)
            v.wait_ge(S["t0"], 2 + NHL)
            v.tensor_copy(bqeff[:, 0:ncbl], pb[7][:, 0:ncbl])
            v.sem_inc(S["bar"], 1)
            v.wait_ge(S["bar"], 5)
            for tb in range(ntb):
                v.wait_ge(S["x1le" if tb % 2 == 0 else "x1lo"], 16 * (tb // 2 + 1))
                for j in range(NBS):
                    bs = v.bn_stats(stats[:, j * 6:(j + 1) * 6],
                                    x_tiles[tb % 2][:, j * 512:(j + 1) * 512])
                bs.then_inc(S["v1a"], 1)
                v.wait_ge(S["v1a"], 2 * tb + 1)
                v.bn_aggr(mv[:, 0:2], stats[:, 0:6 * NBS]).then_inc(S["v1a"], 1)
                v.wait_ge(S["s1q"], tb + 1)
                v.reciprocal(rstd[:, 0:1], stdv[:, 0:1]).then_inc(S["v1r"], 1)
                v.wait_ge(S["v1r"], tb + 1)
                if tb >= 2:
                    v.wait_ge(S["t1"], (tb - 1) * NG)
                v.tensor_scalar(xn_tiles[tb % 2][:], x_tiles[tb % 2][:],
                                mv[:, 0:1], rstd[:, 0:1],
                                op0=ALU.subtract, op1=ALU.mult
                                ).then_inc(S["v1n"], 1)
            v.sem_inc(S["bar"], 1)
            v.wait_ge(S["bar"], 10)
            idx = 0
            sidx = 0
            for cb in range(ncbl):
                for tc in range(NQC):
                    v.wait_ge(S["t2"], idx + 1)
                    if idx >= 2:
                        v.wait_ge(S["t2g"], idx - 1)
                    v.tensor_copy(qraw[:, (idx % 2) * TCQ:(idx % 2 + 1) * TCQ],
                                  pb[idx % 2][:, 0:TCQ]).then_inc(S["v2a"], 1)
                    v.wait_ge(S["t2g"], idx + 1)
                    if sidx >= 4:
                        v.wait_ge(S[f"w2s{sidx % 4}"], 16 * (sidx // 4))
                    v.tensor_scalar(
                        stage[:, (sidx % 4) * 512:(sidx % 4) * 512 + TCQ],
                        pb[2 + idx % 2][:, 0:TCQ], bqeff[:, cb:cb + 1], None,
                        op0=ALU.add).then_inc(S["e2"], 1)
                    idx += 1
                    sidx += 1
            for cb in range(ncbl):
                for tc in range(NTCK):
                    v.wait_ge(S["t2"], idx + 1)
                    if sidx >= 4:
                        v.wait_ge(S[f"w2s{sidx % 4}"], 16 * (sidx // 4))
                    v.tensor_scalar(
                        stage[:, (sidx % 4) * 512:(sidx % 4) * 512 + TCK],
                        pb[idx % 2][:, 0:TCK], bkp_s[:, cb:cb + 1], None,
                        op0=ALU.add).then_inc(S["e2"], 1)
                    idx += 1
                    sidx += 1
            for ch in range(NVCHL):
                for tb in range(ntb):
                    v.wait_ge(S["t2"], idx + 1)
                    if sidx >= 4:
                        v.wait_ge(S[f"w2s{sidx % 4}"], 16 * (sidx // 4))
                    v.tensor_copy(
                        stage[:, (sidx % 4) * 512:(sidx % 4) * 512 + VCH],
                        pb[idx % 2][:, 0:VCH]).then_inc(S["e2"], 1)
                    idx += 1
                    sidx += 1
            v.sem_inc(S["bar"], 1)
            v.wait_ge(S["bar"], 15)
            v.wait_ge(S["t2"], E2T)
            for hb in range(2):
                vv = v_sb[:, hb * ntb * VW:(hb + 1) * ntb * VW] \
                    .rearrange("p (m c) -> p m c", m=ntb)
                v.memset(vv[:, :, HCH:HCH + 1], 1.0).then_inc(S["m3"], 1)
            qcg = -1
            for vh, h, qcs in ATT_VH:
                for qc in qcs:
                    qcg += 1
                    for qb in range(QBC):
                        og = qcg * QBC + qb
                        v.wait_ge(S["t3o"], og + 1)
                        if qcg >= 1:
                            v.wait_ge(S["s3n"], (qcg - 1) * QBC + qb + 1)
                        v.reciprocal(recip[:, qb:qb + 1],
                                     pb[2 + qb][:, HCH:HCH + 1]
                                     ).then_inc(S["v3r"], 1)
            v.sem_inc(S["bar"], 1)
            def v4_evac(fidx):
                v.wait_ge(S["t4f"], fidx + 1)
                v.wait_ge(S[f"a4c{fidx % 4}"], 16 * (fidx // 4 + 1))
                if fidx >= 2:
                    v.wait_ge(S["g4f"], fidx - 1)
                v.tensor_tensor(t1_st[:, (fidx % 2) * OCH:(fidx % 2 + 1) * OCH],
                                pb[fidx % 2][:, 0:OCH],
                                xa_sl[:, (fidx % 4) * OCH:(fidx % 4 + 1) * OCH],
                                op=ALU.add).then_inc(S["v4f"], 1)

            for ps in range(2):
                base = ps * FIP
                fpos = 0
                pend_e = []
                for kind, a1, a2 in SCHED:
                    if kind == "chain":
                        tb = a1
                        gtb = ps * ntbf + tb
                        g2 = gtb % 2
                        v.wait_ge(S["x4l"], 48 * (gtb + 1))
                        if gtb >= 1:
                            v.wait_ge(S["v4n"], gtb)
                            v.wait_ge(S["s4a"], 2 * gtb)
                        v.tensor_tensor(xmid[:], x4_tile[:], xa4_tile[:], op=ALU.add
                                        ).then_inc(S["v4m"], 1)
                        v.wait_ge(S["s4a"], 2 * (gtb + 1))
                        v.tensor_scalar(mv[:, 4 + 2 * g2:5 + 2 * g2],
                                        st4[:, 4 * g2:4 * g2 + 1], 1.0 / D, None,
                                        op0=ALU.mult).then_inc(S["v4a"], 1)
                        v.wait_ge(S["v4a"], 3 * gtb + 1)
                        v.tensor_scalar(st4[:, 4 * g2 + 2:4 * g2 + 3],
                                        mv[:, 4 + 2 * g2:5 + 2 * g2],
                                        mv[:, 4 + 2 * g2:5 + 2 * g2], None,
                                        op0=ALU.mult).then_inc(S["v4a"], 1)
                        v.wait_ge(S["v4a"], 3 * gtb + 2)
                        v.scalar_tensor_tensor(mv[:, 5 + 2 * g2:6 + 2 * g2],
                                               st4[:, 4 * g2 + 1:4 * g2 + 2], 1.0 / D,
                                               st4[:, 4 * g2 + 2:4 * g2 + 3],
                                               op0=ALU.mult,
                                               op1=ALU.subtract).then_inc(S["v4a"], 1)
                        v.wait_ge(S["s4q"], gtb + 1)
                        v.reciprocal(rstd[:, 2 + g2:3 + g2], stdv[:, 2 + g2:3 + g2]
                                     ).then_inc(S["v4r"], 1)
                        v.wait_ge(S["v4r"], gtb + 1)
                        if gtb >= 2:
                            v.wait_ge(S["t4t"], (gtb - 1) * NG)
                        v.tensor_scalar(xn2_tiles[g2][:], xmid[:],
                                        mv[:, 4 + 2 * g2:5 + 2 * g2],
                                        rstd[:, 2 + g2:3 + g2],
                                        op0=ALU.subtract, op1=ALU.mult
                                        ).then_inc(S["v4n"], 1)
                        for f_ in pend_e:
                            v4_evac(f_)
                        pend_e = []
                        v.wait_ge(S["t4r"], gtb + 1)
                        if gtb >= 1:
                            v.wait_ge(S["s4g"], gtb - 1)
                        cb_ = gtb * CPT
                        sA = slice(g2 * NE, (g2 + 1) * NE)
                        v.reduce_max(m1s[:, 4 * g2:4 * g2 + 1], pb[2][:, 0:NE], axis=AX.X
                                     ).then_inc(S["v4c"], 1)
                        v.wait_ge(S["v4c"], cb_ + 1)
                        v.tensor_scalar(is1s[:, sA], pb[2][:, 0:NE],
                                        m1s[:, 4 * g2:4 * g2 + 1], None,
                                        op0=ALU.is_ge).then_inc(S["v4c"], 1)
                        v.wait_ge(S["v4c"], cb_ + 2)
                        v.scalar_tensor_tensor(r2s[:, sA], is1s[:, sA], -1e30,
                                               pb[2][:, 0:NE],
                                               op0=ALU.mult, op1=ALU.add
                                               ).then_inc(S["v4c"], 1)
                        v.wait_ge(S["v4c"], cb_ + 3)
                        v.reduce_max(m1s[:, 4 * g2 + 1:4 * g2 + 2], r2s[:, sA], axis=AX.X
                                     ).then_inc(S["v4c"], 1)
                        v.wait_ge(S["v4c"], cb_ + 4)
                        v.tensor_tensor(m1s[:, 4 * g2 + 2:4 * g2 + 3],
                                        m1s[:, 4 * g2:4 * g2 + 1],
                                        m1s[:, 4 * g2 + 1:4 * g2 + 2],
                                        op=ALU.subtract).then_inc(S["v4c"], 1)
                        v.wait_ge(S["v4c"], cb_ + 5)
                        v.tensor_scalar(is2s[:, sA], r2s[:, sA],
                                        m1s[:, 4 * g2 + 1:4 * g2 + 2], None,
                                        op0=ALU.is_ge).then_inc(S["v4c"], 1)
                        v.wait_ge(S["s4g"], gtb + 1)
                        v.tensor_scalar(w12[:, 2 * g2 + 1:2 * g2 + 2],
                                        w12[:, 2 * g2:2 * g2 + 1], -1.0, 1.0,
                                        op0=ALU.mult, op1=ALU.add).then_inc(S["v4c"], 1)
                        v.wait_ge(S["v4c"], cb_ + 7)
                        v.tensor_scalar(is1s[:, sA], is1s[:, sA],
                                        w12[:, 2 * g2:2 * g2 + 1], None,
                                        op0=ALU.mult).then_inc(S["v4c"], 1)
                        v.wait_ge(S["v4c"], cb_ + 8)
                        if gtb >= 1:
                            v.wait_ge(S["t4c"], gtb)
                        v.scalar_tensor_tensor(comb[:, g2 * NE:(g2 + 1) * NE],
                                               is2s[:, sA],
                                               w12[:, 2 * g2 + 1:2 * g2 + 2],
                                               is1s[:, sA],
                                               op0=ALU.mult, op1=ALU.add
                                               ).then_inc(S["v4c"], 1)
                        v.wait_ge(S["t4c"], gtb + 1)
                        v.tensor_copy(combT[0:NE, tb * 128:(tb + 1) * 128],
                                      pb[3][0:NE, 0:128]).then_inc(S["v4ct"], 1)
                    elif kind == "chunk":
                        c = a1
                        ce = ps * NQCF + c
                        v.wait_ge(S["t4x"], ce + 1)
                        v.tensor_copy(combexp[:, c * TCF:(c + 1) * TCF],
                                      pb[4][:, 0:TCF]).then_inc(S["v4x"], 1)
                        v.wait_ge(S["t4h"], ce + 1)
                        v.wait_ge(S["v4x"], ce + 1)
                        v.tensor_tensor(hwT[:, c * TCF:(c + 1) * TCF],
                                        pb[5][:, 0:TCF],
                                        combexp[:, c * TCF:(c + 1) * TCF],
                                        op=ALU.mult).then_inc(S["v4h"], 1)
                    else:
                        pend_e.append(base + fpos)
                        fpos += 1
                for f_ in pend_e:
                    v4_evac(f_)

        @block.scalar
        def _(sc):
            sc.sem_inc(S["bar"], 1)
            sc.wait_ge(S["bar"], 5)
            for tb in range(ntb):
                sc.wait_ge(S["v1a"], 2 * (tb + 1))
                sc.activation(stdv[:, 0:1], mv[:, 1:2], AF.Sqrt, bias=ceps_s[:, 0:1]
                              ).then_inc(S["s1q"], 1)
                for gp in range(NG):
                    gi = tb * NG + gp
                    sc.wait_ge(S["t1"], gi + 1)
                    sc.activation(
                        xn1T[:, gp * 4:gp * 4 + 4, tb * 128:(tb + 1) * 128],
                        pb[gi % 2][:].rearrange("p (a b) -> p a b", a=4),
                        AF.Copy).then_inc(S["s1e"], 1)
            sc.sem_inc(S["bar"], 1)
            sc.wait_ge(S["bar"], 10)
            sc.sem_inc(S["bar"], 1)
            sc.wait_ge(S["bar"], 15)
            qcg = -1
            for vh, h, qcs in ATT_VH:
                for qc in qcs:
                    qcg += 1
                    for mb in range(ntb):
                        li = qcg * ntb + mb
                        sc.wait_ge(S["t3l"], li + 1)
                        if qcg >= 2:
                            sc.wait_ge(S["t3o"], QBC * (qcg - 1))
                        sc.activation(
                            attnT[:, (qcg % 2) * ntb * TCQ + mb * TCQ:
                                  (qcg % 2) * ntb * TCQ + (mb + 1) * TCQ],
                            pb[(0, 1, 6, 7)[li % 4]][:, 0:TCQ], AF.Exp,
                            bias=cshift_s[:, 0:1]).then_inc(S["s3e"], 1)
                    for qb in range(QBC):
                        og = qcg * QBC + qb
                        sc.wait_ge(S["v3r"], og + 1)
                        sl = og % 4
                        if og >= 4:
                            sc.wait_ge(S[f"w3s{sl}"], 16 * (og // 4))
                        sc.activation(
                            xstg[:, sl * HCH:(sl + 1) * HCH],
                            pb[2 + qb][:, 0:HCH], AF.Copy,
                            scale=recip[:, qb:qb + 1]).then_inc(S["s3n"], 1)
                        sc.wait_ge(S["s3n"], og + 1)
                        tb = qc * QBC + qb
                        sc.dma_start(
                            xap[tb, :, h * HCH:(h + 1) * HCH],
                            xstg[:, sl * HCH:(sl + 1) * HCH]
                        ).then_inc(S[f"w3s{sl}"], 16)
            sc.sem_inc(S["bar"], 1)
            for j in range(4):
                sc.wait_ge(S[f"w3s{j}"],
                           16 * (TOG // 4 + (1 if TOG % 4 > j else 0)))
            def s4_ydma(fidx, och, gtb):
                sc.wait_ge(S["g4f"], fidx + 1)
                sc.dma_start(
                    y[gtb * 128:(gtb + 1) * 128, och * OCH:(och + 1) * OCH],
                    out_st[:, (fidx % 2) * OCH:(fidx % 2 + 1) * OCH]
                ).then_inc(S[f"y4s{fidx % 2}"], 16)

            for ps in range(2):
                base = ps * FIP
                fpos = 0
                pend_y = []
                for kind, a1, a2 in SCHED:
                    if kind == "chain":
                        tb = a1
                        gtb = ps * ntbf + tb
                        g2 = gtb % 2
                        sc.wait_ge(S["v4m"], gtb + 1)
                        if gtb >= 2:
                            sc.wait_ge(S["v4a"], 3 * (gtb - 1))
                        sc.activation(xa4_tile[:], xmid[:], AF.Copy,
                                      accum_out=st4[:, 4 * g2:4 * g2 + 1]
                                      ).then_inc(S["s4a"], 1)
                        sc.wait_ge(S["s4a"], 2 * gtb + 1)
                        sc.activation(xa4_tile[:], xmid[:], AF.Square,
                                      accum_out=st4[:, 4 * g2 + 1:4 * g2 + 2]
                                      ).then_inc(S["s4a"], 1)
                        sc.wait_ge(S["v4a"], 3 * (gtb + 1))
                        if gtb >= 2:
                            sc.wait_ge(S["v4r"], gtb - 1)
                        sc.activation(stdv[:, 2 + g2:3 + g2],
                                      mv[:, 5 + 2 * g2:6 + 2 * g2],
                                      AF.Sqrt, bias=ceps_s[:, 0:1]).then_inc(S["s4q"], 1)
                        if ps == 1 and tb == 0:
                            sc.wait_ge(S["t4f"], FIP)
                        for gp in range(NG):
                            gi = gtb * NG + gp
                            sc.wait_ge(S["t4t"], gi + 1)
                            sc.activation(
                                xn2T[:, gp * 4:gp * 4 + 4, tb * 128:(tb + 1) * 128],
                                pb[6 + gi % 2][:].rearrange("p (a b) -> p a b", a=4),
                                AF.Copy).then_inc(S["s4e"], 1)
                        sc.wait_ge(S["v4c"], gtb * CPT + 5)
                        sc.activation(w12[:, 2 * g2:2 * g2 + 1],
                                      m1s[:, 4 * g2 + 2:4 * g2 + 3], AF.Sigmoid
                                      ).then_inc(S["s4g"], 1)
                        for f_, o_, g_ in pend_y:
                            s4_ydma(f_, o_, g_)
                        pend_y = []
                    elif kind == "ffn":
                        och, tb = a1, a2
                        pend_y.append((base + fpos, och, ps * ntbf + tb))
                        fpos += 1
                for f_, o_, g_ in pend_y:
                    s4_ydma(f_, o_, g_)
            for j in range(2):
                sc.wait_ge(S[f"y4s{j}"], 16 * (FI // 2 + (1 if FI % 2 > j else 0)))

    return nc


def host_layout(cfg, p, Wq, Wk, Wv, W_base, bq, bk, bv, b_base, bias_weight,
                W_router, A, B):
    D, CW, OW = cfg["D"], cfg["CW"], cfg["OW"]
    nk, ncbl = cfg["nk"], cfg["ncbl"]
    NVCHL, VCH, NOCH, OCH = cfg["NVCHL"], cfg["VCH"], cfg["NOCH"], cfg["OCH"]
    NHL = cfg["NHL"]
    chs = slice(p * CW, (p + 1) * CW)
    osl = slice(p * OW, (p + 1) * OW)

    def tile_wt(WT, nch, chw):
        return np.ascontiguousarray(
            WT.reshape(nk, 128, nch, chw).transpose(2, 1, 0, 3))

    out = {}
    out["wqt"] = tile_wt(np.ascontiguousarray(Wq.T[:, chs]), ncbl, 128)
    out["wkt"] = tile_wt(np.ascontiguousarray(Wk.T[:, chs]), ncbl, 128)
    out["wvt"] = tile_wt(np.ascontiguousarray(Wv.T[:, chs]), NVCHL, VCH)
    out["wbt"] = tile_wt(np.ascontiguousarray(W_base.T[:, osl]), NOCH, OCH)
    A_allT = np.ascontiguousarray(A.reshape(cfg["NE"] * cfg["R"], D).T)
    out["at"] = np.ascontiguousarray(
        A_allT.reshape(nk, 128, 128).transpose(1, 0, 2))
    out["bt"] = np.ascontiguousarray(
        B.transpose(0, 2, 1).reshape(cfg["NE"] * cfg["R"], D)[:, osl])
    WrT = np.ascontiguousarray(W_router.T)
    out["wrt"] = np.ascontiguousarray(
        WrT.reshape(nk, 128, cfg["NE"]).transpose(1, 0, 2))
    out["bqp"] = np.ascontiguousarray(bq[chs].reshape(ncbl, 128).T)
    out["bkp"] = np.ascontiguousarray(bk[chs].reshape(ncbl, 128).T)
    bvz = np.zeros((128, CW), dtype=np.float32)
    bvz[0] = bv[chs]
    out["bvr"] = bvz
    bbz = np.zeros((128, OW), dtype=np.float32)
    bbz[0] = b_base[osl]
    out["bbr"] = bbz
    out["bwt"] = np.ascontiguousarray(bias_weight[p * NHL:(p + 1) * NHL].T)
    out.update(host_consts(cfg))
    return out


def kernel(L0_attn_x, L0_attn_Wq, L0_attn_bq, L0_attn_Wk, L0_attn_bk,
           L0_attn_Wv, L0_attn_bv, L0_attn_bias_weight, L0_ffn_W_base,
           L0_ffn_b_base, L0_ffn_W_router, L0_ffn_A, L0_ffn_B):
    cfg = make_cfg(D=4096, L=2048, NH=16, CD=32)
    OW = cfg["OW"]
    x = np.asarray(L0_attn_x)
    Bz, L, D = x.shape

    args = (np.asarray(L0_attn_Wq), np.asarray(L0_attn_Wk),
            np.asarray(L0_attn_Wv), np.asarray(L0_ffn_W_base),
            np.asarray(L0_attn_bq), np.asarray(L0_attn_bk),
            np.asarray(L0_attn_bv), np.asarray(L0_ffn_b_base),
            np.asarray(L0_attn_bias_weight), np.asarray(L0_ffn_W_router),
            np.asarray(L0_ffn_A), np.asarray(L0_ffn_B))
    shared = [host_layout(cfg, p, *args) for p in range(2)]

    in_maps = []
    for c in range(8):
        b, p = c // 2, c % 2
        m = dict(shared[p])
        m["xc"] = np.ascontiguousarray(x[b])
        m["xres"] = np.ascontiguousarray(x[b][:, p * OW:(p + 1) * OW])
        in_maps.append(m)

    nc = build_program(cfg)
    res = run_bass_kernel_spmd(nc, in_maps, core_ids=list(range(8)))

    out = np.empty((Bz, L, D), dtype=np.float32)
    for c in range(8):
        b, p = c // 2, c % 2
        out[b, :, p * OW:(p + 1) * OW] = res.results[c]["y"]
    return out
